# version 04
# baseline (speedup 1.0000x reference)
import math
from contextlib import ExitStack

import numpy as np
import ml_dtypes

import concourse.bass as bass
import concourse.mybir as mybir
from concourse.bass_utils import run_bass_kernel_spmd

F32 = mybir.dt.float32
BF16 = mybir.dt.bfloat16
AF = mybir.ActivationFunctionType
ALU = mybir.AluOpType
AXX = mybir.AxisListType.X

NCORE = 8
B, S, D, NH, DH, II = 4, 2048, 2048, 16, 128, 8192
NKC = D // 128
NIC = II // 128
NTT = S // 128
NQT = 8
TOWN = NQT * 128
ISQ = 1.0 / math.sqrt(DH)
EPS = 1e-5
NEG = -1.0e30
SWAP_MASK = list(range(16, 32)) + list(range(16))



def _local_tile_order(h):
    return [2 * i + h for i in range(NQT)] + [2 * i + (1 - h) for i in range(NQT)]


def _prep_shared(Wqkv, bqkv, Wd, bd, Wg, Wu, Wdn):
    out = {}
    w3 = Wqkv.reshape(D, NH, 3 * DH)

    def tile_w(mat, noc):
        return np.ascontiguousarray(
            mat.reshape(NKC, 128, noc, 128).transpose(2, 1, 0, 3))

    out["wq"] = tile_w(np.ascontiguousarray(w3[:, :, 0:128]).reshape(D, NH * 128), NH)
    out["wk"] = tile_w(np.ascontiguousarray(w3[:, :, 128:256]).reshape(D, NH * 128), NH)
    out["wv"] = tile_w(np.ascontiguousarray(w3[:, :, 256:384]).reshape(D, NH * 128), NH)
    out["wd"] = tile_w(Wd, NKC)
    g = np.arange(NIC * 128)
    c, j = (g // 128) // 4, (g // 128) % 4
    f = g % 128
    ffn_cols = 4 * (128 * c + f) + j
    out["wg"] = tile_w(np.ascontiguousarray(Wg[:, ffn_cols]), NIC)
    out["wu"] = tile_w(np.ascontiguousarray(Wu[:, ffn_cols]), NIC)
    out["wdn"] = np.ascontiguousarray(Wdn[ffn_cols, :].reshape(NIC, 128, D))
    bias = np.zeros((128, 64), dtype=np.float32)
    b3 = bqkv.reshape(NH, 3 * DH)
    bias[:, 0:NH] = b3[:, 0:128].T
    bias[:, 16:32] = b3[:, 128:256].T
    bias[:, 32:48] = b3[:, 256:384].T
    bias[:, 48:64] = bd.reshape(NKC, 128).T
    out["bias"] = np.ascontiguousarray(bias)
    out["ident"] = np.eye(128, dtype=np.float32)
    out["_tri"] = np.where(np.triu(np.ones((128, 128), np.float32), 1) > 0,
                           np.float32(NEG), np.float32(0))
    return out


def _prep_core(xb, h, shared):
    order = _local_tile_order(h)
    xt = np.ascontiguousarray(
        xb.T.reshape(NKC, 128, NTT, 128).transpose(2, 1, 0, 3)[order])
    xo = np.ascontiguousarray(xt[0:NQT].transpose(2, 1, 0, 3))
    tglob = np.concatenate([np.arange(128 * g, 128 * (g + 1)) for g in order]).astype(np.float64)
    freqs = 10000.0 ** (-np.arange(16, dtype=np.float64) / 16.0)
    ang = tglob[None, :] * freqs[:, None]
    cc = np.cos(ang).astype(np.float32)
    ss = np.sin(ang).astype(np.float32)
    trig = np.zeros((128, 2 * S), dtype=np.float32)
    trig[0:16, 0:S] = cc
    trig[16:32, 0:S] = cc
    trig[0:16, S:2 * S] = -ss
    trig[16:32, S:2 * S] = ss
    mask = np.zeros((128, 256), dtype=np.float32)
    mask[:, 0:128] = shared["_tri"]
    if h == 0:
        mask[:, 128:256] = NEG
    return {"xt": xt, "xo": xo, "trig": np.ascontiguousarray(trig),
            "mask": np.ascontiguousarray(mask)}



class _Sem:
    __slots__ = ("h", "n")

    def __init__(self, h):
        self.h = h
        self.n = 0


def _bump(instr, sem, k=1):
    instr.then_inc(sem.h, k)
    sem.n += k
    return sem.n


def _w(eng, cond):
    sem, val = cond
    if val > 0:
        eng.wait_ge(sem.h, val)


def _build_inner(upto=9, dbg=()):
    nc = bass.Bass("TRN2", target_bir_lowering=False, dynamic_dma_scratch_size=16384)
    ap = {}
    ap["xt"] = nc.dram_tensor("xt", [NTT, 128, NKC, 128], F32, kind="ExternalInput").ap()
    ap["xo"] = nc.dram_tensor("xo", [NKC, 128, NQT, 128], F32, kind="ExternalInput").ap()
    for k, noc in (("wq", NH), ("wk", NH), ("wv", NH), ("wd", NKC), ("wg", NIC), ("wu", NIC)):
        ap[k] = nc.dram_tensor(k, [noc, 128, NKC, 128], F32, kind="ExternalInput").ap()
    ap["wdn"] = nc.dram_tensor("wdn", [NIC, 128, D], F32, kind="ExternalInput").ap()
    ap["bias"] = nc.dram_tensor("bias", [128, 64], F32, kind="ExternalInput").ap()
    ap["trig"] = nc.dram_tensor("trig", [128, 2 * S], F32, kind="ExternalInput").ap()
    ap["mask"] = nc.dram_tensor("mask", [128, 256], F32, kind="ExternalInput").ap()
    ap["ident"] = nc.dram_tensor("ident", [128, 128], F32, kind="ExternalInput").ap()
    ap["out"] = nc.dram_tensor("out", [NKC, 128, NQT, 128], F32, kind="ExternalOutput").ap()
    x2kind = "ExternalOutput" if "x2" in dbg else "Internal"
    ap["x2s"] = nc.dram_tensor("x2s", [NKC, 128, TOWN], F32, kind=x2kind).ap()
    dbg_bufs = {}
    for name in dbg:
        if name != "x2":
            dbg_bufs[name] = nc.dram_tensor("dbg_" + name, [128, 65536], BF16,
                                            kind="ExternalOutput").ap()

    with ExitStack() as ctx:
        E = ctx.enter_context
        A = E(nc.sbuf_tensor("arena_a", [128, 32768], BF16))
        Bb = E(nc.sbuf_tensor("arena_b", [128, 32768], BF16))
        C = E(nc.sbuf_tensor("arena_c", [128, 16384], BF16))
        Dd = E(nc.sbuf_tensor("arena_d", [128, 16384], BF16))
        SLAB = [E(nc.sbuf_tensor(f"slab{i}", [128, 2048], BF16)) for i in range(3)]
        PB = [E(nc.psum_tensor(f"pb{i}", [128, 512], F32)) for i in range(8)]

        LD = _Sem(E(nc.semaphore("LD")))
        WLD = _Sem(E(nc.semaphore("WLD")))
        X2S = _Sem(E(nc.semaphore("X2S")))
        SO = _Sem(E(nc.semaphore("SO")))
        PE = _Sem(E(nc.semaphore("PE")))
        VE = _Sem(E(nc.semaphore("VE")))
        SE = _Sem(E(nc.semaphore("SE")))
        PL = _Sem(E(nc.semaphore("PL")))
        block = E(nc.Block())

        Bf = Bb.ap().bitcast(F32)
        Cf = C.ap().bitcast(F32)
        Df = Dd.ap().bitcast(F32)
        ln1 = A.ap()
        ln1_k = ln1.rearrange("p (kc t) -> p kc t", kc=NKC)

        def bcols(o, w):
            return Bb.ap()[:, o:o + w]
        Qb = [bcols(0, 1024), bcols(1024, 1024)]
        Kb = [bcols(2048, 2048), bcols(4096, 2048)]
        Vfb = [bcols(6144, 2048), bcols(8192, 2048)]
        Vtb = [bcols(10240, 2048), bcols(12288, 2048)]
        Pstr = [bcols(14336, 2048), bcols(16384, 2048)]
        PTb = [bcols(18432, 2048), bcols(20480, 2048)]
        ropescr = bcols(22528, 2048)
        bias_v = Bf[:, 12288:12352]
        ident_v = bcols(24832, 128)
        onescol = bcols(24960, 2)[:, 0:1]
        onesrow = Bf[0:1, 12544:12672]
        rows_m = Bf[0:1, 13696:14720]
        rows_r = Bf[0:1, 14720:15744]
        t1s = [Bf[:, 0:2048], Bf[:, 2048:4096]]
        dense_fm = Bb.ap()[:, 0:16384]
        x2b = Bb.ap()[:, 0:16384]
        x2b_k = x2b.rearrange("p (kc t) -> p kc t", kc=NKC)
        xoc_st = [Bf[:, 8192:9216], Bf[:, 9216:10240]]
        t2s = [Bf[:, 10240:10752], Bf[:, 10752:11264]]
        sq_oc = [bcols(22528, 1024), bcols(23552, 1024)]
        xst = [Cf[:, 0:2048], Cf[:, 2048:4096]]
        xbt = [C.ap()[:, 8192:10240], C.ap()[:, 10240:12288]]
        sqt = [C.ap()[:, 12288:14336], C.ap()[:, 14336:16384]]
        attn_fm = C.ap()[:, 0:16384]
        abs_t = [C.ap()[:, 1024 * k:1024 * (k + 1)] for k in range(4)]
        sp_m1 = C.ap()[:, 4096:5120]
        sp_m2 = C.ap()[:, 5120:6144]
        sp_n1 = C.ap()[:, 6144:7168]
        sp_n2 = C.ap()[:, 7168:8192]
        sp_thr = C.ap()[:, 8192:9216]
        sil = [C.ap()[:, 9216:9728], C.ap()[:, 9728:10240]]
        down_fm = C.ap()[:, 0:16384]
        strip = [Df[:, 0:2048], Df[:, 2048:4096]]
        trig_v = Dd.ap()[:, 8192:12288]
        mask_v = Df[:, 6144:6400]
        negmax = [Df[:, 6400:6401], Df[:, 6401:6402]]
        nscale = [Df[:, 6402:6403], Df[:, 6403:6404]]
        sums = [Df[:, 6404:6405], Df[:, 6405:6406]]
        recs = [Df[:, 6406:6407], Df[:, 6407:6408]]
        lrows = [Df[0:1, 6408:6920], Df[0:1, 6920:7432]]
        ln2 = Dd.ap()[:, 0:16384]
        ln2_k = ln2.rearrange("p (kc t) -> p kc t", kc=NKC)
        rows4_m = Df[0:1, 0:1024]
        rows4_r = Df[0:1, 1024:2048]
        x2st = [Df[:, 2048:3072], Df[:, 3072:4096]]
        t4s = [Df[:, 4096:4608], Df[:, 4608:5120]]
        outst = [Df[:, 5120:6144], Df[:, 6144:7168]]
        sq4 = [Dd.ap()[:, 14336:15360], Dd.ap()[:, 15360:16384]]
        ones4c = Dd.ap()[:, 14080:14082][:, 0:1]
        ones4r = Df[0:1, 7040:7168]

        bank_free = [(VE, 0)] * 8
        bank_full = [0] * 8
        slab_free = [(PE, 0)] * 3
        slab_full = [0] * 3
        slab_idx = [0]

        def load_slab(g, src):
            s = slab_idx[0] % 3
            _w(g, slab_free[s])
            d = g.dma_start(out=SLAB[s][:, :], in_=src)
            slab_full[s] = _bump(d, WLD, 16)
            slab_idx[0] += 1
            return s

        def mm_bank():
            b = mm_bank.i % 4
            mm_bank.i += 1
            return b
        mm_bank.i = 0

        def tr_bank():
            b = 5 if (tr_bank.i % 2 == 0) else 6
            tr_bank.i += 1
            return b
        tr_bank.i = 0

        def wslab(oc_key, idx):
            return ap[oc_key][idx].rearrange("p kc f -> p (kc f)")

        def g_pre(g):
            d = g.dma_start(out=trig_v, in_=ap["trig"])
            _bump(d, WLD, 16)
            d = g.dma_start(out=ident_v, in_=ap["ident"])
            _bump(d, WLD, 16)
            g.memset(onescol, 1.0).then_inc(PL.h, 1)
            g.memset(onesrow, 1.0).then_inc(PL.h, 1)
            PL.n += 2
            for key in ("wq", "wk", "wv"):
                load_slab(g, wslab(key, 0))
        block.gpsimd(g_pre)
        ones_ready = PL.n
        tables_ld = 32

        def s_pre(sp):
            _bump(sp.dma_start(out=bias_v, in_=ap["bias"]), LD, 16)
            _bump(sp.dma_start(out=mask_v, in_=ap["mask"]), LD, 16)
        block.sync(s_pre)
        consts_ld = LD.n

        xfree = [(VE, 0), (VE, 0)]
        sfree = [(PE, 0), (PE, 0)]
        rowsbank = [4, 7]
        lrows_free = [(PE, 0), (PE, 0)]
        xload = [0] * NTT
        ln1_done = 0

        for tl in range(NTT):
            r = tl % 2

            def s_ld(sp, tl=tl, r=r):
                _w(sp, xfree[r])
                src = ap["xt"][tl].rearrange("p kc tau -> p (kc tau)")
                d = sp.dma_start(out=xst[r], in_=src)
                xload[tl] = _bump(d, LD, 16)
            block.sync(s_ld)

            def v_cast(v, tl=tl, r=r):
                v.wait_ge(LD.h, xload[tl])
                _w(v, sfree[r])
                _bump(v.tensor_copy(xbt[r], xst[r]), VE)
            block.vector(v_cast)
            cast_done = VE.n

            def g_sq(g, tl=tl, r=r):
                g.wait_ge(LD.h, xload[tl])
                _w(g, sfree[r])
                _bump(g.tensor_tensor(sqt[r], xst[r], xst[r], ALU.mult), PL)
            block.gpsimd(g_sq)
            sq_done = PL.n

            rb = rowsbank[r]

            def t_stats(t, r=r, rb=rb, tl=tl, cast_done=cast_done, sq_done=sq_done):
                t.wait_ge(VE.h, cast_done)
                t.wait_ge(PL.h, sq_done)
                if tl < 2:
                    t.wait_ge(PL.h, ones_ready)
                _w(t, bank_free[rb])
                xb3 = xbt[r].rearrange("p (kc tau) -> p kc tau", kc=NKC)
                sq3 = sqt[r].rearrange("p (kc tau) -> p kc tau", kc=NKC)
                ins = None
                for kc in range(NKC):
                    ins = t.matmul(PB[rb][0:1, 0:128], onescol, xb3[:, kc],
                                   start=(kc == 0), stop=False)
                for kc in range(NKC):
                    ins = t.matmul(PB[rb][0:1, 128:256], onescol, sq3[:, kc],
                                   start=False, stop=(kc == NKC - 1))
                bank_full[rb] = _bump(ins, PE)
            block.tensor(t_stats)
            sfree[r] = (PE, PE.n)

            lr = lrows[r]
            m_r, v_r, r_r, mr_r = lr[:, 0:128], lr[:, 128:256], lr[:, 256:384], lr[:, 384:512]

            def v_rows(v, rb=rb, r=r, m_r=m_r, v_r=v_r, mr_r=mr_r):
                v.wait_ge(PE.h, bank_full[rb])
                _w(v, lrows_free[r])
                v.tensor_scalar(m_r, PB[rb][0:1, 0:128], 1.0 / D, None, ALU.mult)
                v.tensor_scalar(v_r, PB[rb][0:1, 128:256], 1.0 / D, None, ALU.mult)
                v.tensor_tensor(mr_r, m_r, m_r, ALU.mult)
                v.tensor_tensor(v_r, v_r, mr_r, ALU.subtract)
                ins = v.tensor_scalar(v_r, v_r, float(EPS), None, ALU.add)
                _bump(ins, VE)
            block.vector(v_rows)
            var_done = VE.n
            bank_free[rb] = (VE, var_done)

            def s_rstd(s, v_r=v_r, r_r=r_r, var_done=var_done):
                s.wait_ge(VE.h, var_done)
                s.activation(v_r, v_r, AF.Ln)
                _bump(s.activation(r_r, v_r, AF.Exp, scale=-0.5), SE)
            block.scalar(s_rstd)
            rstd_done = SE.n

            def v_mr(v, m_r=m_r, r_r=r_r, mr_r=mr_r, rstd_done=rstd_done):
                v.wait_ge(SE.h, rstd_done)
                _bump(v.tensor_tensor(mr_r, m_r, r_r, ALU.mult), VE)
            block.vector(v_mr)
            rows_done = VE.n

            bc = tr_bank()

            def t_bc(t, bc=bc, mr_r=mr_r, r_r=r_r, rows_done=rows_done):
                t.wait_ge(VE.h, rows_done)
                _w(t, bank_free[bc])
                t.matmul(PB[bc][:, 0:128], onesrow, mr_r, start=True, stop=False)
                ins = t.matmul(PB[bc][:, 128:256], onesrow, r_r, start=False, stop=True)
                bank_full[bc] = _bump(ins, PE)
            block.tensor(t_bc)
            lrows_free[r] = (PE, PE.n)

            def v_norm(v, tl=tl, r=r, bc=bc):
                v.wait_ge(PE.h, bank_full[bc])
                x3 = xst[r].rearrange("p (kc tau) -> p kc tau", kc=NKC)
                t3 = t1s[r].rearrange("p (kc tau) -> p kc tau", kc=NKC)
                rbc = PB[bc][:, 128:256].rearrange("p (o tau) -> p o tau", o=1)\
                    .broadcast_to([128, NKC, 128])
                mbc = PB[bc][:, 0:128].rearrange("p (o tau) -> p o tau", o=1)\
                    .broadcast_to([128, NKC, 128])
                _bump(v.tensor_tensor(t3, x3, rbc, ALU.mult), VE)
                xfree[r] = (VE, VE.n)
                ins = v.tensor_tensor(ln1_k[:, :, tl * 128:tl * 128 + 128], t3, mbc,
                                      ALU.subtract)
                _bump(ins, VE)
            block.vector(v_norm)
            bank_free[bc] = (VE, VE.n)
            ln1_done = VE.n

        if "ln1" in dbg or upto == 0:
            _emit_dbg(block, dbg_bufs.get("ln1"), [ln1[:, :]],
                      [LD, WLD, PE, VE, SE, PL], SO)
            if upto == 0:
                return nc

        avbank = [7, 4]
        strip_free = [(SE, 0), (SE, 0)]
        pst_free = [(PE, 0), (PE, 0)]
        rope_done = 0
        attn_evac = 0
        for hh in range(NH):
            pp = hh % 2
            myslabs = {"q": 0, "k": 1, "v": 2}
            fill_list = ([("q", f) for f in range(2)]
                         + [("k", f) for f in range(4)]
                         + [("v", f) for f in range(4)])
            for fi, (kind, f) in enumerate(fill_list):
                s = myslabs[kind]

                def t_fill(t, hh=hh, kind=kind, f=f, s=s, fi=fi):
                    if hh == 0 and fi == 0:
                        t.wait_ge(VE.h, ln1_done)
                    t.wait_ge(WLD.h, slab_full[s])
                    b = mm_bank()
                    _w(t, bank_free[b])
                    ins = None
                    for kc in range(NKC):
                        ins = t.matmul(PB[b][:, 0:512],
                                       SLAB[s][:, kc * 128:kc * 128 + 128],
                                       ln1_k[:, kc, f * 512:f * 512 + 512],
                                       start=(kc == 0), stop=(kc == NKC - 1))
                    bank_full[b] = _bump(ins, PE)
                    t_fill.b = b
                block.tensor(t_fill)
                b = t_fill.b

                def s_evac(s_, hh=hh, pp=pp, kind=kind, f=f, b=b, fi=fi):
                    if hh == 0 and fi == 0:
                        s_.wait_ge(LD.h, consts_ld)
                    s_.wait_ge(PE.h, bank_full[b])
                    if kind == "q":
                        dst = Qb[pp][:, f * 512:f * 512 + 512]
                        bcol = bias_v[:, hh:hh + 1]
                    elif kind == "k":
                        dst = Kb[pp][:, f * 512:f * 512 + 512]
                        bcol = bias_v[:, 16 + hh:17 + hh]
                    else:
                        dst = Vfb[pp][:, f * 512:f * 512 + 512]
                        bcol = bias_v[:, 32 + hh:33 + hh]
                    _bump(s_.activation(dst, PB[b][:, 0:512], AF.Identity, bias=bcol), SE)
                block.scalar(s_evac)
                bank_free[b] = (SE, SE.n)
                if fi in (1, 5, 9):
                    slab_free[myslabs[kind]] = (PE, PE.n)
            qkv_evac = SE.n

            def g_slab(g, hh=hh):
                if hh + 1 < NH:
                    for key in ("wq", "wk", "wv"):
                        load_slab(g, wslab(key, hh + 1))
            block.gpsimd(g_slab)

            prev_rope = rope_done

            def v_shuf(v, pp=pp, qkv_evac=qkv_evac, prev_rope=prev_rope, hh=hh):
                v.wait_ge(SE.h, qkv_evac)
                if prev_rope:
                    v.wait_ge(PL.h, prev_rope)
                if hh == 0:
                    v.wait_ge(WLD.h, tables_ld)
                v.stream_shuffle(ropescr[0:32, 0:1024], Qb[pp][0:32, :], SWAP_MASK)
                ins = v.stream_shuffle(ropescr[0:32, 1024:2048], Kb[pp][0:32, 0:1024],
                                       SWAP_MASK)
                _bump(ins, VE)
            block.vector(v_shuf)
            shuf1 = VE.n

            def g_rope(g, pp=pp, shuf1=shuf1):
                g.wait_ge(VE.h, shuf1)
                cc = trig_v[0:32, 0:2048]
                ss = trig_v[0:32, 2048:4096]
                sco = ropescr[0:32, 0:1024]
                g.tensor_tensor(sco, sco, ss[:, 0:1024], ALU.mult)
                g.tensor_tensor(Qb[pp][0:32, :], Qb[pp][0:32, :], cc[:, 0:1024], ALU.mult)
                g.tensor_tensor(Qb[pp][0:32, :], Qb[pp][0:32, :], sco, ALU.add)
                sck = ropescr[0:32, 1024:2048]
                g.tensor_tensor(sck, sck, ss[:, 0:1024], ALU.mult)
                g.tensor_tensor(Kb[pp][0:32, 0:1024], Kb[pp][0:32, 0:1024],
                                cc[:, 0:1024], ALU.mult)
                ins = g.tensor_tensor(Kb[pp][0:32, 0:1024], Kb[pp][0:32, 0:1024],
                                      sck, ALU.add)
                _bump(ins, PL)
            block.gpsimd(g_rope)
            kh1_done = PL.n

            def v_shuf2(v, pp=pp, kh1_done=kh1_done):
                v.wait_ge(PL.h, kh1_done)
                ins = v.stream_shuffle(ropescr[0:32, 0:1024], Kb[pp][0:32, 1024:2048],
                                       SWAP_MASK)
                _bump(ins, VE)
            block.vector(v_shuf2)
            shuf2 = VE.n

            def g_rope2(g, pp=pp, shuf2=shuf2):
                g.wait_ge(VE.h, shuf2)
                cc = trig_v[0:32, 0:2048]
                ss = trig_v[0:32, 2048:4096]
                sck = ropescr[0:32, 0:1024]
                g.tensor_tensor(sck, sck, ss[:, 1024:2048], ALU.mult)
                g.tensor_tensor(Kb[pp][0:32, 1024:2048], Kb[pp][0:32, 1024:2048],
                                cc[:, 1024:2048], ALU.mult)
                ins = g.tensor_tensor(Kb[pp][0:32, 1024:2048], Kb[pp][0:32, 1024:2048],
                                      sck, ALU.add)
                _bump(ins, PL)
            block.gpsimd(g_rope2)
            rope_done = PL.n

            for tp in range(NTT // 2):
                pair = []

                def t_vtr(t, pp=pp, qkv_evac=qkv_evac, hh=hh, tp=tp, pair=pair):
                    if tp == 0:
                        t.wait_ge(SE.h, qkv_evac)
                        if hh == 0:
                            t.wait_ge(WLD.h, tables_ld)
                    for tt in (2 * tp, 2 * tp + 1):
                        b = tr_bank()
                        _w(t, bank_free[b])
                        ins = t.transpose(PB[b].bitcast(BF16)[:, 0:128],
                                          Vfb[pp][:, tt * 128:tt * 128 + 128], ident_v)
                        bank_full[b] = _bump(ins, PE)
                        pair.append((tt, b))
                block.tensor(t_vtr)

                def v_vtr(v, pp=pp, pair=pair):
                    for tt, b in pair:
                        v.wait_ge(PE.h, bank_full[b])
                        ins = v.tensor_copy(Vtb[pp][:, tt * 128:tt * 128 + 128],
                                            PB[b].bitcast(BF16)[:, 0:128])
                        _bump(ins, VE)
                        bank_free[b] = (VE, VE.n)
                block.vector(v_vtr)

            score_banks = [[] for _ in range(NQT)]
            strip_ready = [0] * NQT
            strip_sready = [0] * NQT
            exp_done = [0] * NQT
            norm_done = [0] * NQT
            rd = rope_done

            def stageA(i, hh=hh, pp=pp, rd=rd):
                r = i % 2
                W = (i + 1) * 128

                def t_sc(t, i=i, r=r, W=W):
                    t.wait_ge(PL.h, rd)
                    _w(t, strip_free[r])
                    for span, base in ((0, 0), (1, 1024)):
                        for ob in range(0, W, 512):
                            n = min(512, W - ob)
                            b = mm_bank()
                            _w(t, bank_free[b])
                            ins = t.matmul(PB[b][:, 0:n],
                                           Qb[pp][:, i * 128:i * 128 + 128],
                                           Kb[pp][:, base + ob:base + ob + n],
                                           start=True, stop=True)
                            bank_full[b] = _bump(ins, PE)
                            score_banks[i].append((span, ob, n, b))
                block.tensor(t_sc)

                def s_ev(s, i=i, r=r, W=W):
                    for span, ob, n, b in score_banks[i]:
                        s.wait_ge(PE.h, bank_full[b])
                        w_un = n - 128 if (ob + n == W) else n
                        if w_un > 0:
                            lo = span * W + ob
                            _bump(s.activation(strip[r][:, lo:lo + w_un],
                                               PB[b][:, 0:w_un], AF.Copy), SE)
                    strip_sready[i] = SE.n
                block.scalar(s_ev)

                def v_mask(v, i=i, r=r, W=W, hh=hh):
                    if hh == 0 and i == 0:
                        v.wait_ge(LD.h, consts_ld)
                    for span, ob, n, b in score_banks[i]:
                        if ob + n == W:
                            v.wait_ge(PE.h, bank_full[b])
                            lo = span * W + ob
                            mk = mask_v[:, 0:128] if span == 0 else mask_v[:, 128:256]
                            ins = v.tensor_tensor(strip[r][:, lo + n - 128:lo + n],
                                                  PB[b][:, n - 128:n], mk, ALU.add)
                            _bump(ins, VE)
                    strip_ready[i] = VE.n
                block.vector(v_mask)
                for span, ob, n, b in score_banks[i]:
                    bank_free[b] = ((VE, strip_ready[i]) if (ob + n == W)
                                    else (SE, strip_sready[i]))

            def stageB(i):
                r = i % 2
                W = (i + 1) * 128

                def v_max(v, i=i, r=r, W=W):
                    v.wait_ge(SE.h, strip_sready[i])
                    v.wait_ge(VE.h, strip_ready[i])
                    v.reduce_max(negmax[r], strip[r][:, 0:2 * W], axis=AXX, negate=True)
                    ins = v.tensor_scalar(nscale[r], negmax[r], ISQ, None, ALU.mult)
                    _bump(ins, VE)
                block.vector(v_max)
                mx = VE.n

                def s_exp(s, i=i, r=r, W=W, mx=mx):
                    s.wait_ge(VE.h, mx)
                    _w(s, pst_free[r])
                    ins = s.activation(Pstr[r][:, 0:2 * W], strip[r][:, 0:2 * W],
                                       AF.Exp, bias=nscale[r][:, 0:1], scale=ISQ,
                                       accum_out=sums[r][:, 0:1])
                    _bump(ins, SE)
                block.scalar(s_exp)
                exp_done[i] = SE.n
                strip_free[r] = (SE, SE.n)

                def v_nrm(v, i=i, r=r, W=W):
                    v.wait_ge(SE.h, exp_done[i])
                    _bump(v.reciprocal(recs[r], sums[r]), VE)
                    v.wait_ge(VE.h, VE.n)
                    ins = v.tensor_scalar(Pstr[r][:, 0:2 * W], Pstr[r][:, 0:2 * W],
                                          recs[r][:, 0:1], None, ALU.mult)
                    _bump(ins, VE)
                block.vector(v_nrm)
                norm_done[i] = VE.n

            def stageC(i, hh=hh, pp=pp):
                r = i % 2
                nt = i + 1
                for jp in range(nt):
                    pair = []

                    def t_pt(t, i=i, r=r, jp=jp, pair=pair):
                        if jp == 0:
                            t.wait_ge(VE.h, norm_done[i])
                        for j in (2 * jp, 2 * jp + 1):
                            b = tr_bank()
                            _w(t, bank_free[b])
                            ins = t.transpose(PB[b].bitcast(BF16)[:, 0:128],
                                              Pstr[r][:, j * 128:j * 128 + 128], ident_v)
                            bank_full[b] = _bump(ins, PE)
                            pair.append((j, b))
                    block.tensor(t_pt)

                    def v_pt(v, r=r, pair=pair):
                        for j, b in pair:
                            v.wait_ge(PE.h, bank_full[b])
                            ins = v.tensor_copy(PTb[r][:, j * 128:j * 128 + 128],
                                                PB[b].bitcast(BF16)[:, 0:128])
                            _bump(ins, VE)
                            bank_free[b] = (VE, VE.n)
                    block.vector(v_pt)
                pt_ev = VE.n

                ab = avbank[i % 2]

                def t_av(t, i=i, r=r, nt=nt, ab=ab, pt_ev=pt_ev, pp=pp):
                    t.wait_ge(VE.h, pt_ev)
                    _w(t, bank_free[ab])
                    vts = list(range(nt)) + [8 + k for k in range(nt)]
                    ins = None
                    for j, vt in enumerate(vts):
                        ins = t.matmul(PB[ab][:, 0:128],
                                       Vtb[pp][:, vt * 128:vt * 128 + 128],
                                       PTb[r][:, j * 128:j * 128 + 128],
                                       start=(j == 0), stop=(j == 2 * nt - 1))
                    bank_full[ab] = _bump(ins, PE)
                block.tensor(t_av)
                pst_free[r] = (PE, PE.n)

                def v_av(v, i=i, ab=ab, hh=hh):
                    v.wait_ge(PE.h, bank_full[ab])
                    dst = attn_fm[:, hh * 1024 + i * 128:hh * 1024 + i * 128 + 128]
                    ins = v.tensor_copy(dst, PB[ab][:, 0:128])
                    _bump(ins, VE)
                    bank_free[ab] = (VE, VE.n)
                block.vector(v_av)

            if hh == 0 and "qk0" in dbg:
                _emit_dbg(block, dbg_bufs.get("qk0"),
                          [Qb[0][:, :], Kb[0][:, :], Vtb[0][:, :]],
                          [LD, WLD, PE, VE, SE, PL], SO)
                return nc

            stageA(0)
            for i in range(NQT):
                if i + 1 < NQT:
                    stageA(i + 1)
                stageB(i)
                stageC(i)
                if hh == 0 and i == 2 and "sm0" in dbg:
                    _emit_dbg(block, dbg_bufs.get("sm0"),
                              [Dd.ap()[:, 0:8192], Pstr[0][:, :], Pstr[1][:, :],
                               PTb[0][:, :], Dd.ap()[:, 12800:12832]],
                              [LD, WLD, PE, VE, SE, PL], SO)
                    return nc
            attn_evac = VE.n

        if "attn" in dbg or upto == 1:
            _emit_dbg(block, dbg_bufs.get("attn"), [attn_fm],
                      [LD, WLD, PE, VE, SE, PL], SO)
            if upto == 1:
                return nc

        dn_banks = {}
        sqfree = [0, 0]
        dstat_pe = [0] * NKC
        for oc in range(NKC):
            def g_wd(g, oc=oc):
                load_slab(g, wslab("wd", oc))
            block.gpsimd(g_wd)
            s = (slab_idx[0] - 1) % 3

            def t_dn(t, oc=oc, s=s):
                if oc == 0:
                    t.wait_ge(VE.h, attn_evac)
                t.wait_ge(WLD.h, slab_full[s])
                for tb in range(2):
                    b = mm_bank()
                    _w(t, bank_free[b])
                    ins = None
                    for kc in range(NKC):
                        src = attn_fm[:, kc * 1024 + tb * 512:kc * 1024 + tb * 512 + 512]
                        ins = t.matmul(PB[b][:, 0:512],
                                       SLAB[s][:, kc * 128:kc * 128 + 128], src,
                                       start=(kc == 0), stop=(kc == NKC - 1))
                    bank_full[b] = _bump(ins, PE)
                    dn_banks[(oc, tb)] = b
            block.tensor(t_dn)
            slab_free[s] = (PE, PE.n)

            def s_dnev(s_, oc=oc):
                if oc < 2:
                    s_.wait_ge(VE.h, attn_evac)
                for tb in range(2):
                    b = dn_banks[(oc, tb)]
                    s_.wait_ge(PE.h, bank_full[b])
                    dst = dense_fm[:, oc * 1024 + tb * 512:oc * 1024 + tb * 512 + 512]
                    _bump(s_.activation(dst, PB[b][:, 0:512], AF.Identity,
                                        bias=bias_v[:, 48 + oc:49 + oc]), SE)
            block.scalar(s_dnev)
            dnev = SE.n
            for tb in range(2):
                bank_free[dn_banks[(oc, tb)]] = (SE, dnev)

            def g_dsq(g, oc=oc, dnev=dnev):
                g.wait_ge(SE.h, dnev)
                _w(g, (PE, sqfree[oc % 2]))
                do = dense_fm[:, oc * 1024:oc * 1024 + 1024]
                _bump(g.tensor_tensor(sq_oc[oc % 2], do, do, ALU.mult), PL)
            block.gpsimd(g_dsq)
            dsqd = PL.n

            def t_dst(t, oc=oc, dnev=dnev, dsqd=dsqd):
                t.wait_ge(SE.h, dnev)
                t.wait_ge(PL.h, dsqd)
                if oc == 0:
                    for b in range(4, 8):
                        _w(t, bank_free[b])
                do = dense_fm[:, oc * 1024:oc * 1024 + 1024]
                sq = sq_oc[oc % 2]
                ins = None
                for th in range(2):
                    ins = t.matmul(PB[4 + th][0:1, 0:512], onescol,
                                   do[:, th * 512:th * 512 + 512],
                                   start=(oc == 0), stop=(oc == NKC - 1))
                for th in range(2):
                    ins = t.matmul(PB[6 + th][0:1, 0:512], onescol,
                                   sq[:, th * 512:th * 512 + 512],
                                   start=(oc == 0), stop=(oc == NKC - 1))
                _bump(ins, PE)
            block.tensor(t_dst)
            sqfree[oc % 2] = PE.n
            dstat_pe[oc] = PE.n
        dstat_done = PE.n

        def v_drows(v):
            v.wait_ge(PE.h, dstat_done)
            ins = None
            for th in range(2):
                m = rows_m[:, th * 512:th * 512 + 512]
                r_ = rows_r[:, th * 512:th * 512 + 512]
                v.tensor_scalar(m, PB[4 + th][0:1, 0:512], 1.0 / D, None, ALU.mult)
                v.tensor_scalar(r_, PB[6 + th][0:1, 0:512], 1.0 / D, None, ALU.mult)
                t_ = lrows[0][:, 0:512]
                v.tensor_tensor(t_, m, m, ALU.mult)
                v.tensor_tensor(r_, r_, t_, ALU.subtract)
                ins = v.tensor_scalar(r_, r_, float(EPS), None, ALU.add)
            _bump(ins, VE)
        block.vector(v_drows)
        drow_v = VE.n
        for b in range(4, 8):
            bank_free[b] = (VE, drow_v)

        def s_drstd(s):
            s.wait_ge(VE.h, drow_v)
            s.activation(rows_r, rows_r, AF.Ln)
            _bump(s.activation(rows_r, rows_r, AF.Exp, scale=-0.5), SE)
        block.scalar(s_drstd)
        drstd = SE.n

        def v_dmr(v):
            v.wait_ge(SE.h, drstd)
            _bump(v.tensor_tensor(rows_m, rows_m, rows_r, ALU.mult), VE)
        block.vector(v_dmr)
        dmr = VE.n

        def t_dbc(t):
            t.wait_ge(VE.h, dmr)
            ins = None
            for th in range(2):
                _w(t, bank_free[4 + th])
                _w(t, bank_free[6 + th])
                t.matmul(PB[4 + th][:, 0:512], onesrow,
                         rows_m[:, th * 512:th * 512 + 512], start=True, stop=True)
                ins = t.matmul(PB[6 + th][:, 0:512], onesrow,
                               rows_r[:, th * 512:th * 512 + 512], start=True, stop=True)
            _bump(ins, PE)
        block.tensor(t_dbc)
        dbc = PE.n
        for b in range(4, 8):
            bank_full[b] = dbc

        x2w = [0] * NKC
        x2c_prev = [0, 0]
        for oc in range(NKC):
            r = oc % 2

            def s_xo(sp, oc=oc, r=r):
                if oc < 2:
                    sp.wait_ge(VE.h, attn_evac)
                else:
                    sp.wait_ge(PL.h, x2c_prev[r])
                src = ap["xo"][oc].rearrange("p i tau -> p (i tau)")
                d = sp.dma_start(out=xoc_st[r], in_=src)
                x2w[oc] = 0
                s_xo.ld = _bump(d, LD, 16)
            block.sync(s_xo)
            xold = s_xo.ld

            def v_x2(v, oc=oc, r=r, xold=xold):
                v.wait_ge(PE.h, dbc)
                if oc >= 1:
                    v.wait_ge(PL.h, x2c_prev[1 - r])
                do = dense_fm[:, oc * 1024:oc * 1024 + 1024]
                ins = None
                for th in range(2):
                    t_ = t2s[th]
                    v.tensor_tensor(t_, do[:, th * 512:th * 512 + 512],
                                    PB[6 + th][:, 0:512], ALU.mult)
                    ins = v.tensor_tensor(t_, t_, PB[4 + th][:, 0:512], ALU.subtract)
                _bump(ins, VE)
            block.vector(v_x2)
            tv = VE.n

            def g_x2(g, oc=oc, r=r, tv=tv, xold=xold):
                g.wait_ge(VE.h, tv)
                g.wait_ge(LD.h, xold)
                ins = None
                for th in range(2):
                    ins = g.tensor_tensor(xoc_st[r][:, th * 512:th * 512 + 512],
                                          xoc_st[r][:, th * 512:th * 512 + 512],
                                          t2s[th], ALU.add)
                _bump(ins, PL)
            block.gpsimd(g_x2)
            x2c_prev[r] = PL.n
            x2c = PL.n

            def s_x2w(sp, oc=oc, r=r, x2c=x2c):
                sp.wait_ge(PL.h, x2c)
                d = sp.dma_start(out=ap["x2s"][oc], in_=xoc_st[r])
                x2w[oc] = _bump(d, X2S, 16)
            block.sync(s_x2w)
        p2_pool_done = PL.n

        if "x2" in dbg or upto == 2:
            _emit_dbg(block, None, [], [LD, WLD, PE, VE, SE, PL, X2S], SO)
            if upto == 2:
                return nc

        x2ld = [0] * NKC
        v25free = [0, 0]
        s25free = [0, 0]
        t25free = [0, 0]
        for oc in range(NKC):
            r = oc % 2

            def s_x2r(sp, oc=oc, r=r):
                sp.wait_ge(X2S.h, x2w[oc])
                if oc < 2:
                    sp.wait_ge(PL.h, p2_pool_done)
                else:
                    sp.wait_ge(VE.h, v25free[r])
                    sp.wait_ge(PL.h, s25free[r])
                d = sp.dma_start(out=xoc_st[r], in_=ap["x2s"][oc])
                x2ld[oc] = _bump(d, LD, 16)
            block.sync(s_x2r)

            def v_x2c(v, oc=oc, r=r):
                v.wait_ge(LD.h, x2ld[oc])
                v.wait_ge(PE.h, dstat_pe[oc])
                _bump(v.tensor_copy(x2b_k[:, oc, :], xoc_st[r]), VE)
            block.vector(v_x2c)
            v25free[r] = VE.n
            x2cast = VE.n

            def g_sq2(g, oc=oc, r=r):
                g.wait_ge(LD.h, x2ld[oc])
                if oc < 2:
                    g.wait_ge(PE.h, dstat_done)
                else:
                    g.wait_ge(PE.h, t25free[r])
                _bump(g.tensor_tensor(sq_oc[r], xoc_st[r], xoc_st[r], ALU.mult), PL)
            block.gpsimd(g_sq2)
            s25free[r] = PL.n
            sq2 = PL.n

            def t_2st(t, oc=oc, r=r, x2cast=x2cast, sq2=sq2):
                t.wait_ge(VE.h, x2cast)
                t.wait_ge(PL.h, sq2)
                if oc == 0:
                    for b in range(4):
                        _w(t, bank_free[b])
                ins = None
                for th in range(2):
                    ins = t.matmul(PB[th][0:1, 0:512], onescol,
                                   x2b_k[:, oc, th * 512:th * 512 + 512],
                                   start=(oc == 0), stop=(oc == NKC - 1))
                for th in range(2):
                    ins = t.matmul(PB[2 + th][0:1, 0:512], onescol,
                                   sq_oc[r][:, th * 512:th * 512 + 512],
                                   start=(oc == 0), stop=(oc == NKC - 1))
                _bump(ins, PE)
            block.tensor(t_2st)
            t25free[r] = PE.n
        stat2 = PE.n

        def v_2rows(v):
            v.wait_ge(PE.h, stat2)
            ins = None
            for th in range(2):
                m = rows_m[:, th * 512:th * 512 + 512]
                r_ = rows_r[:, th * 512:th * 512 + 512]
                v.tensor_scalar(m, PB[th][0:1, 0:512], 1.0 / D, None, ALU.mult)
                v.tensor_scalar(r_, PB[2 + th][0:1, 0:512], 1.0 / D, None, ALU.mult)
                t_ = lrows[0][:, 0:512]
                v.tensor_tensor(t_, m, m, ALU.mult)
                v.tensor_tensor(r_, r_, t_, ALU.subtract)
                ins = v.tensor_scalar(r_, r_, float(EPS), None, ALU.add)
            _bump(ins, VE)
        block.vector(v_2rows)
        r2v = VE.n
        for b in range(4):
            bank_free[b] = (VE, r2v)

        def s_2rstd(s):
            s.wait_ge(VE.h, r2v)
            s.activation(rows_r, rows_r, AF.Ln)
            _bump(s.activation(rows_r, rows_r, AF.Exp, scale=-0.5), SE)
        block.scalar(s_2rstd)
        r2s = SE.n

        def v_2mr(v):
            v.wait_ge(SE.h, r2s)
            _bump(v.tensor_tensor(rows_m, rows_m, rows_r, ALU.mult), VE)
        block.vector(v_2mr)
        mr2 = VE.n

        def t_2bc(t):
            t.wait_ge(VE.h, mr2)
            ins = None
            for th in range(2):
                _w(t, bank_free[4 + th])
                _w(t, bank_free[6 + th])
                t.matmul(PB[4 + th][:, 0:512], onesrow,
                         rows_m[:, th * 512:th * 512 + 512], start=True, stop=True)
                ins = t.matmul(PB[6 + th][:, 0:512], onesrow,
                               rows_r[:, th * 512:th * 512 + 512], start=True, stop=True)
            _bump(ins, PE)
        block.tensor(t_2bc)
        bc2 = PE.n

        for oc in range(NKC):
            def v_ln2(v, oc=oc):
                v.wait_ge(PE.h, bc2)
                ins = None
                for th in range(2):
                    dst = ln2_k[:, oc, th * 512:th * 512 + 512]
                    v.tensor_tensor(dst, x2b_k[:, oc, th * 512:th * 512 + 512],
                                    PB[6 + th][:, 0:512], ALU.mult)
                    ins = v.tensor_tensor(dst, dst, PB[4 + th][:, 0:512], ALU.subtract)
                _bump(ins, VE)
            block.vector(v_ln2)
        ln2_done = VE.n
        for b in range(4, 8):
            bank_free[b] = (VE, ln2_done)

        if "ln2" in dbg or upto == 3:
            _emit_dbg(block, dbg_bufs.get("ln2"), [ln2],
                      [LD, WLD, PE, VE, SE, PL, X2S], SO)
            if upto == 3:
                return nc

        hlo = A.ap()
        hhi = Bb.ap()

        def hview(ci):
            return (hlo if ci < 32 else hhi)[:, (ci % 32) * 1024:(ci % 32) * 1024 + 1024]

        gu_banks = {}
        sp_done = [0]
        silfree = [0, 0]
        for ci in range(NIC):
            def g_wgu(g, ci=ci):
                load_slab(g, wslab("wg", ci))
                load_slab(g, wslab("wu", ci))
            block.gpsimd(g_wgu)
            sg = (slab_idx[0] - 2) % 3
            su = (slab_idx[0] - 1) % 3

            def t_gu(t, ci=ci, sg=sg, su=su):
                if ci == 0:
                    t.wait_ge(VE.h, ln2_done)
                for tb in range(2):
                    for kind, s in (("g", sg), ("u", su)):
                        t.wait_ge(WLD.h, slab_full[s])
                        b = mm_bank()
                        _w(t, bank_free[b])
                        ins = None
                        for kc in range(NKC):
                            ins = t.matmul(PB[b][:, 0:512],
                                           SLAB[s][:, kc * 128:kc * 128 + 128],
                                           ln2_k[:, kc, tb * 512:tb * 512 + 512],
                                           start=(kc == 0), stop=(kc == NKC - 1))
                        bank_full[b] = _bump(ins, PE)
                        gu_banks[(ci, tb, kind)] = b
            block.tensor(t_gu)
            slab_free[sg] = (PE, PE.n)
            slab_free[su] = (PE, PE.n)
            gu_pe = PE.n

            def s_sil(s, ci=ci):
                for tb in range(2):
                    b = gu_banks[(ci, tb, "g")]
                    s.wait_ge(PE.h, bank_full[b])
                    if ci >= 1:
                        s.wait_ge(VE.h, silfree[tb])
                    _bump(s.activation(sil[tb], PB[b][:, 0:512], AF.Silu), SE)
            block.scalar(s_sil)
            sil_done = SE.n

            def v_h(v, ci=ci, sil_done=sil_done):
                hv = hview(ci)
                if ci == 0:
                    v.wait_ge(PE.h, bc2)
                if ci == 32:
                    v.wait_ge(X2S.h, x2w[NKC - 1])
                ins = None
                for tb in range(2):
                    bg = gu_banks[(ci, tb, "g")]
                    bu = gu_banks[(ci, tb, "u")]
                    v.wait_ge(SE.h, sil_done)
                    v.wait_ge(PE.h, bank_full[bu])
                    ins = v.tensor_tensor(hv[:, tb * 512:tb * 512 + 512],
                                          sil[tb], PB[bu][:, 0:512], ALU.mult)
                    _bump(ins, VE)
                    silfree[tb] = VE.n
                    bank_free[bg] = (SE, sil_done)
                    bank_free[bu] = (VE, VE.n)
            block.vector(v_h)
            h_done = VE.n

            if ci % 4 == 3 and "hraw" not in dbg:
                c4 = ci // 4

                def g_abs(g, c4=c4, h_done=h_done):
                    g.wait_ge(VE.h, h_done)
                    for jj in range(4):
                        hv = hview(4 * c4 + jj)
                        _bump(g.tensor_tensor(abs_t[jj], hv, hv, ALU.mult), PL)
                block.gpsimd(g_abs)
                absd = PL.n

                def v_sp(v, absd=absd):
                    v.wait_ge(PL.h, absd)
                    v.tensor_tensor(sp_m1, abs_t[0], abs_t[1], ALU.max)
                    v.tensor_tensor(sp_n1, abs_t[0], abs_t[1], ALU.min)
                    v.tensor_tensor(sp_m2, abs_t[2], abs_t[3], ALU.max)
                    v.tensor_tensor(sp_n2, abs_t[2], abs_t[3], ALU.min)
                    v.tensor_tensor(sp_m1, sp_m1, sp_m2, ALU.min)
                    v.tensor_tensor(sp_n1, sp_n1, sp_n2, ALU.max)
                    v.tensor_tensor(sp_thr, sp_m1, sp_n1, ALU.max)
                    ins = None
                    for jj in range(4):
                        ins = v.tensor_tensor(abs_t[jj], abs_t[jj], sp_thr, ALU.is_ge)
                    _bump(ins, VE)
                block.vector(v_sp)
                spv = VE.n

                def g_sp(g, c4=c4, spv=spv):
                    g.wait_ge(VE.h, spv)
                    ins = None
                    for jj in range(4):
                        hv = hview(4 * c4 + jj)
                        ins = g.tensor_tensor(hv, hv, abs_t[jj], ALU.mult)
                        _bump(ins, PL)
                block.gpsimd(g_sp)
                sp_done[0] = PL.n
        h_all = PL.n
        gu_all_pe = PE.n

        if "h" in dbg or "hraw" in dbg or upto == 4:
            _emit_dbg(block, dbg_bufs.get("h") or dbg_bufs.get("hraw"),
                      [hlo[:, :], hhi[:, :]],
                      [LD, WLD, PE, VE, SE, PL, X2S], SO)
            if upto == 4:
                return nc

        for p4 in range(4):
            ocs = list(range(4 * p4, 4 * p4 + 4))
            for ci in range(NIC):
                def g_wdn(g, ci=ci):
                    load_slab(g, ap["wdn"][ci])
                block.gpsimd(g_wdn)
                s = (slab_idx[0] - 1) % 3

                def t_dwn(t, p4=p4, ci=ci, s=s, ocs=ocs):
                    if p4 == 0 and ci == 0:
                        t.wait_ge(PL.h, h_all)
                    t.wait_ge(WLD.h, slab_full[s])
                    ins = None
                    for k, oc in enumerate(ocs):
                        for tb in range(2):
                            b = 2 * k + tb
                            if ci == 0:
                                _w(t, bank_free[b])
                            ins = t.matmul(PB[b][:, 0:512],
                                           SLAB[s][:, oc * 128:oc * 128 + 128],
                                           hview(ci)[:, tb * 512:tb * 512 + 512],
                                           start=(ci == 0), stop=(ci == NIC - 1))
                            if ci == NIC - 1:
                                bank_full[b] = _bump(ins, PE)
                    if ci != NIC - 1:
                        _bump(ins, PE)
                block.tensor(t_dwn)
                slab_free[s] = (PE, PE.n)

            def v_dwev(v, p4=p4, ocs=ocs):
                for k, oc in enumerate(ocs):
                    for tb in range(2):
                        b = 2 * k + tb
                        v.wait_ge(PE.h, bank_full[b])
                        if p4 == 0 and k == 0 and tb == 0:
                            v.wait_ge(PL.h, sp_done[0])
                        dst = down_fm[:, oc * 1024 + tb * 512:oc * 1024 + tb * 512 + 512]
                        _bump(v.tensor_copy(dst, PB[b][:, 0:512]), VE)
                        bank_free[b] = (VE, VE.n)
            block.vector(v_dwev)
        down_done = VE.n

        def g_ones4(g):
            g.wait_ge(VE.h, down_done)
            g.memset(ones4c, 1.0)
            ins = g.memset(ones4r, 1.0)
            _bump(ins, PL)
        block.gpsimd(g_ones4)
        ones4 = PL.n

        sq4free = [0, 0]
        for oc in range(NKC):
            r = oc % 2

            def g_sq4(g, oc=oc, r=r):
                g.wait_ge(VE.h, down_done)
                if oc >= 2:
                    g.wait_ge(PE.h, sq4free[r])
                do = down_fm[:, oc * 1024:oc * 1024 + 1024]
                _bump(g.tensor_tensor(sq4[r], do, do, ALU.mult), PL)
            block.gpsimd(g_sq4)
            sqd = PL.n

            def t_4st(t, oc=oc, r=r, sqd=sqd):
                t.wait_ge(PL.h, sqd)
                if oc == 0:
                    t.wait_ge(PL.h, ones4)
                    for b in range(4, 8):
                        _w(t, bank_free[b])
                do = down_fm[:, oc * 1024:oc * 1024 + 1024]
                ins = None
                for th in range(2):
                    ins = t.matmul(PB[4 + th][0:1, 0:512], ones4c,
                                   do[:, th * 512:th * 512 + 512],
                                   start=(oc == 0), stop=(oc == NKC - 1))
                for th in range(2):
                    ins = t.matmul(PB[6 + th][0:1, 0:512], ones4c,
                                   sq4[r][:, th * 512:th * 512 + 512],
                                   start=(oc == 0), stop=(oc == NKC - 1))
                _bump(ins, PE)
            block.tensor(t_4st)
            sq4free[r] = PE.n
        stat4 = PE.n

        def v_4rows(v):
            v.wait_ge(PE.h, stat4)
            ins = None
            for th in range(2):
                m = rows4_m[:, th * 512:th * 512 + 512]
                r_ = rows4_r[:, th * 512:th * 512 + 512]
                v.tensor_scalar(m, PB[4 + th][0:1, 0:512], 1.0 / D, None, ALU.mult)
                v.tensor_scalar(r_, PB[6 + th][0:1, 0:512], 1.0 / D, None, ALU.mult)
                t_ = t4s[0][0:1, 0:512]
                v.tensor_tensor(t_, m, m, ALU.mult)
                v.tensor_tensor(r_, r_, t_, ALU.subtract)
                ins = v.tensor_scalar(r_, r_, float(EPS), None, ALU.add)
            _bump(ins, VE)
        block.vector(v_4rows)
        r4v = VE.n

        def s_4rstd(s):
            s.wait_ge(VE.h, r4v)
            s.activation(rows4_r, rows4_r, AF.Ln)
            _bump(s.activation(rows4_r, rows4_r, AF.Exp, scale=-0.5), SE)
        block.scalar(s_4rstd)
        r4s = SE.n

        def v_4mr(v):
            v.wait_ge(SE.h, r4s)
            _bump(v.tensor_tensor(rows4_m, rows4_m, rows4_r, ALU.mult), VE)
        block.vector(v_4mr)
        mr4 = VE.n

        def t_4bc(t):
            t.wait_ge(VE.h, mr4)
            ins = None
            for th in range(2):
                _w(t, bank_free[th])
                _w(t, bank_free[2 + th])
                t.matmul(PB[th][:, 0:512], ones4r,
                         rows4_m[:, th * 512:th * 512 + 512], start=True, stop=True)
                ins = t.matmul(PB[2 + th][:, 0:512], ones4r,
                               rows4_r[:, th * 512:th * 512 + 512], start=True, stop=True)
            _bump(ins, PE)
        block.tensor(t_4bc)
        bc4 = PE.n

        x2ld4 = [0] * NKC
        outw = [0] * NKC
        fg_prev = [0, 0]
        for oc in range(NKC):
            r = oc % 2

            def s_x2r4(sp, oc=oc, r=r):
                sp.wait_ge(X2S.h, x2w[oc])
                if oc < 2:
                    sp.wait_ge(VE.h, down_done)
                else:
                    sp.wait_ge(PL.h, fg_prev[r])
                d = sp.dma_start(out=x2st[r], in_=ap["x2s"][oc])
                x2ld4[oc] = _bump(d, LD, 16)
            block.sync(s_x2r4)

            def v_fin(v, oc=oc, r=r):
                v.wait_ge(PE.h, bc4)
                if oc >= 1:
                    v.wait_ge(PL.h, fg_prev[1 - r])
                if oc >= 2:
                    v.wait_ge(SO.h, outw[oc - 2])
                do = down_fm[:, oc * 1024:oc * 1024 + 1024]
                ins = None
                for th in range(2):
                    t_ = t4s[th]
                    v.tensor_tensor(t_, do[:, th * 512:th * 512 + 512],
                                    PB[2 + th][:, 0:512], ALU.mult)
                    ins = v.tensor_tensor(outst[r][:, th * 512:th * 512 + 512],
                                          t_, PB[th][:, 0:512], ALU.subtract)
                _bump(ins, VE)
            block.vector(v_fin)
            fv = VE.n

            def g_fin(g, oc=oc, r=r, fv=fv):
                g.wait_ge(VE.h, fv)
                g.wait_ge(LD.h, x2ld4[oc])
                _bump(g.tensor_tensor(outst[r], outst[r], x2st[r], ALU.add), PL)
            block.gpsimd(g_fin)
            fg_prev[r] = PL.n
            fg = PL.n

            def s_outw(sp, oc=oc, r=r, fg=fg):
                sp.wait_ge(PL.h, fg)
                dst = ap["out"][oc].rearrange("p i tau -> p (i tau)")
                d = sp.dma_start(out=dst, in_=outst[r])
                outw[oc] = _bump(d, SO, 16)
            block.sync(s_outw)

        final_so = SO.n

        def s_fini(sp):
            sp.wait_ge(SO.h, final_so)
        block.sync(s_fini)

    return nc


def _emit_dbg(block, dbuf, srcs, sems, SO):
    vals = [(s, s.n) for s in sems]

    def s_dbg(sp):
        for s, v in vals:
            if v > 0:
                sp.wait_ge(s.h, v)
        off = 0
        if dbuf is not None:
            for src in srcs:
                w = src.shape[-1]
                _bump(sp.dma_start(out=dbuf[:, off:off + w], in_=src), SO, 16)
                off += w
        if SO.n > 0:
            sp.wait_ge(SO.h, SO.n)
    block.sync(s_dbg)


_ENG_ORDER = {"EngineType.SP": 0, "EngineType.Activation": 1, "EngineType.DVE": 2,
              "EngineType.PE": 3, "EngineType.Pool": 4}


REORDER = [0]


def _build(upto=9, dbg=()):
    nc = _build_inner(upto, dbg)
    if not REORDER[0]:
        return nc
    fn = nc.main_func
    blocks = list(fn.blocks)
    first, last = blocks[0], blocks[-1]
    mid = blocks[1:-1]

    def key(b):
        for i in b.instructions:
            e = str(i.engine)
            if e in _ENG_ORDER:
                return _ENG_ORDER[e]
        return 9
    out = [first]
    W = 120
    for w in range(0, len(mid), W):
        out.extend(sorted(mid[w:w + W], key=key))
    out.append(last)
    fn.blocks = out
    return nc



_NC_CACHE = {}


def _get_nc(upto=9, dbg=()):
    key = (upto, tuple(dbg))
    if key not in _NC_CACHE:
        _NC_CACHE[key] = _build(upto, dbg)
    return _NC_CACHE[key]


def _make_in_maps(x, Wqkv, bqkv, Wd, bd, Wg, Wu, Wdn):
    shared = _prep_shared(Wqkv, bqkv, Wd, bd, Wg, Wu, Wdn)
    wkeys = ["wq", "wk", "wv", "wd", "wg", "wu", "wdn", "bias", "ident"]
    in_maps = []
    for c in range(NCORE):
        b, h = c // 2, c % 2
        core = _prep_core(x[b], h, shared)
        m = {k: shared[k] for k in wkeys}
        m.update({k: core[k] for k in ("xt", "xo", "trig", "mask")})
        in_maps.append(m)
    return in_maps


def kernel(L0_attn_hidden_states, L0_attn_W_qkv, L0_attn_b_qkv, L0_attn_W_dense,
           L0_attn_b_dense, L0_ffn_w_gate, L0_ffn_w_up, L0_ffn_w_down,
           _upto=9, _dbg=()):
    x = np.asarray(L0_attn_hidden_states, dtype=np.float32)
    in_maps = _make_in_maps(
        x,
        np.asarray(L0_attn_W_qkv, np.float32), np.asarray(L0_attn_b_qkv, np.float32),
        np.asarray(L0_attn_W_dense, np.float32), np.asarray(L0_attn_b_dense, np.float32),
        np.asarray(L0_ffn_w_gate, np.float32), np.asarray(L0_ffn_w_up, np.float32),
        np.asarray(L0_ffn_w_down, np.float32))
    nc = _get_nc(_upto, _dbg)
    res = run_bass_kernel_spmd(nc, in_maps, core_ids=list(range(NCORE)))
    if _upto < 9:
        return res
    out = np.zeros((B, S, D), dtype=np.float32)
    for c in range(NCORE):
        b, h = c // 2, c % 2
        o = np.asarray(res.results[c]["out"])
        for i in range(NQT):
            g = 2 * i + h
            out[b, 128 * g:128 * g + 128, :] = \
                o[:, :, i, :].transpose(2, 0, 1).reshape(128, D)
    return out


# version 05 (final)
# speedup vs baseline: 1.0701x; 1.0701x over previous
import math
from contextlib import ExitStack

import numpy as np
import ml_dtypes

import concourse.bass as bass
import concourse.mybir as mybir
from concourse.bass_utils import run_bass_kernel_spmd

F32 = mybir.dt.float32
BF16 = mybir.dt.bfloat16
AF = mybir.ActivationFunctionType
ALU = mybir.AluOpType
AXX = mybir.AxisListType.X

NCORE = 8
B, S, D, NH, DH, II = 4, 2048, 2048, 16, 128, 8192
NKC = D // 128
NIC = II // 128
NTT = S // 128
NQT = 8
TOWN = NQT * 128
ISQ = 1.0 / math.sqrt(DH)
EPS = 1e-5
NEG = -1.0e30
SWAP_MASK = list(range(16, 32)) + list(range(16))



def _local_tile_order(h):
    return [2 * i + h for i in range(NQT)] + [2 * i + (1 - h) for i in range(NQT)]


def _prep_shared(Wqkv, bqkv, Wd, bd, Wg, Wu, Wdn):
    out = {}
    w3 = Wqkv.reshape(D, NH, 3 * DH)

    def tile_w(mat, noc):
        return np.ascontiguousarray(
            mat.reshape(NKC, 128, noc, 128).transpose(2, 1, 0, 3))

    out["wq"] = tile_w(np.ascontiguousarray(w3[:, :, 0:128]).reshape(D, NH * 128), NH)
    out["wk"] = tile_w(np.ascontiguousarray(w3[:, :, 128:256]).reshape(D, NH * 128), NH)
    out["wv"] = tile_w(np.ascontiguousarray(w3[:, :, 256:384]).reshape(D, NH * 128), NH)
    out["wd"] = tile_w(Wd, NKC)
    g = np.arange(NIC * 128)
    c, j = (g // 128) // 4, (g // 128) % 4
    f = g % 128
    ffn_cols = 4 * (128 * c + f) + j
    out["wg"] = tile_w(np.ascontiguousarray(Wg[:, ffn_cols]), NIC)
    out["wu"] = tile_w(np.ascontiguousarray(Wu[:, ffn_cols]), NIC)
    out["wdn"] = np.ascontiguousarray(Wdn[ffn_cols, :].reshape(NIC, 128, D))
    bias = np.zeros((128, 64), dtype=np.float32)
    b3 = bqkv.reshape(NH, 3 * DH)
    bias[:, 0:NH] = b3[:, 0:128].T
    bias[:, 16:32] = b3[:, 128:256].T
    bias[:, 32:48] = b3[:, 256:384].T
    bias[:, 48:64] = bd.reshape(NKC, 128).T
    out["bias"] = np.ascontiguousarray(bias)
    out["ident"] = np.eye(128, dtype=np.float32)
    out["_tri"] = np.where(np.triu(np.ones((128, 128), np.float32), 1) > 0,
                           np.float32(NEG), np.float32(0))
    return out


def _prep_core(xb, h, shared):
    order = _local_tile_order(h)
    xt = np.ascontiguousarray(
        xb.T.reshape(NKC, 128, NTT, 128).transpose(2, 1, 0, 3)[order])
    xo = np.ascontiguousarray(xt[0:NQT].transpose(2, 1, 0, 3))
    tglob = np.concatenate([np.arange(128 * g, 128 * (g + 1)) for g in order]).astype(np.float64)
    freqs = 10000.0 ** (-np.arange(16, dtype=np.float64) / 16.0)
    ang = tglob[None, :] * freqs[:, None]
    cc = np.cos(ang).astype(np.float32)
    ss = np.sin(ang).astype(np.float32)
    trig = np.zeros((128, 2 * S), dtype=np.float32)
    trig[0:16, 0:S] = cc
    trig[16:32, 0:S] = cc
    trig[0:16, S:2 * S] = -ss
    trig[16:32, S:2 * S] = ss
    mask = np.zeros((128, 256), dtype=np.float32)
    mask[:, 0:128] = shared["_tri"]
    if h == 0:
        mask[:, 128:256] = NEG
    return {"xt": xt, "xo": xo, "trig": np.ascontiguousarray(trig),
            "mask": np.ascontiguousarray(mask)}



class _Sem:
    __slots__ = ("h", "n")

    def __init__(self, h):
        self.h = h
        self.n = 0


def _bump(instr, sem, k=1):
    instr.then_inc(sem.h, k)
    sem.n += k
    return sem.n


def _w(eng, cond):
    sem, val = cond
    if val > 0:
        eng.wait_ge(sem.h, val)


def _build_inner(upto=9, dbg=()):
    nc = bass.Bass("TRN2", target_bir_lowering=False, dynamic_dma_scratch_size=16384)
    ap = {}
    ap["xt"] = nc.dram_tensor("xt", [NTT, 128, NKC, 128], F32, kind="ExternalInput").ap()
    ap["xo"] = nc.dram_tensor("xo", [NKC, 128, NQT, 128], F32, kind="ExternalInput").ap()
    for k, noc in (("wq", NH), ("wk", NH), ("wv", NH), ("wd", NKC), ("wg", NIC), ("wu", NIC)):
        ap[k] = nc.dram_tensor(k, [noc, 128, NKC, 128], F32, kind="ExternalInput").ap()
    ap["wdn"] = nc.dram_tensor("wdn", [NIC, 128, D], F32, kind="ExternalInput").ap()
    ap["bias"] = nc.dram_tensor("bias", [128, 64], F32, kind="ExternalInput").ap()
    ap["trig"] = nc.dram_tensor("trig", [128, 2 * S], F32, kind="ExternalInput").ap()
    ap["mask"] = nc.dram_tensor("mask", [128, 256], F32, kind="ExternalInput").ap()
    ap["ident"] = nc.dram_tensor("ident", [128, 128], F32, kind="ExternalInput").ap()
    ap["out"] = nc.dram_tensor("out", [NKC, 128, NQT, 128], F32, kind="ExternalOutput").ap()
    x2kind = "ExternalOutput" if "x2" in dbg else "Internal"
    ap["x2s"] = nc.dram_tensor("x2s", [NKC, 128, TOWN], F32, kind=x2kind).ap()
    dbg_bufs = {}
    for name in dbg:
        if name != "x2":
            dbg_bufs[name] = nc.dram_tensor("dbg_" + name, [128, 65536], BF16,
                                            kind="ExternalOutput").ap()

    with ExitStack() as ctx:
        E = ctx.enter_context
        A = E(nc.sbuf_tensor("arena_a", [128, 32768], BF16))
        Bb = E(nc.sbuf_tensor("arena_b", [128, 32768], BF16))
        C = E(nc.sbuf_tensor("arena_c", [128, 16384], BF16))
        Dd = E(nc.sbuf_tensor("arena_d", [128, 16384], BF16))
        SLAB = [E(nc.sbuf_tensor(f"slab{i}", [128, 2048], BF16)) for i in range(3)]
        PB = [E(nc.psum_tensor(f"pb{i}", [128, 512], F32)) for i in range(8)]

        LD = _Sem(E(nc.semaphore("LD")))
        WLD = _Sem(E(nc.semaphore("WLD")))
        X2S = _Sem(E(nc.semaphore("X2S")))
        SO = _Sem(E(nc.semaphore("SO")))
        PE = _Sem(E(nc.semaphore("PE")))
        VE = _Sem(E(nc.semaphore("VE")))
        SE = _Sem(E(nc.semaphore("SE")))
        PL = _Sem(E(nc.semaphore("PL")))
        block = E(nc.Block())

        Bf = Bb.ap().bitcast(F32)
        Cf = C.ap().bitcast(F32)
        Df = Dd.ap().bitcast(F32)
        ln1 = A.ap()
        ln1_k = ln1.rearrange("p (kc t) -> p kc t", kc=NKC)

        def bcols(o, w):
            return Bb.ap()[:, o:o + w]
        Qb = [bcols(0, 1024), bcols(1024, 1024)]
        Kb = [bcols(2048, 2048), bcols(4096, 2048)]
        Vfb = [bcols(6144, 2048), bcols(8192, 2048)]
        Vtb = [bcols(10240, 2048), bcols(12288, 2048)]
        Pstr = [bcols(14336, 2048), bcols(16384, 2048)]
        PTb = [bcols(18432, 2048), bcols(20480, 2048)]
        ropescr = bcols(22528, 2048)
        bias_v = Bf[:, 12288:12352]
        ident_v = bcols(24832, 128)
        onescol = bcols(24960, 2)[:, 0:1]
        onesrow = Bf[0:1, 12544:12672]
        rows_m = Bf[0:1, 13696:14720]
        rows_r = Bf[0:1, 14720:15744]
        t1s = [Bf[:, 0:2048], Bf[:, 2048:4096]]
        dense_fm = Bb.ap()[:, 0:16384]
        x2b = Bb.ap()[:, 0:16384]
        x2b_k = x2b.rearrange("p (kc t) -> p kc t", kc=NKC)
        xoc_st = [Bf[:, 8192:9216], Bf[:, 9216:10240]]
        t2s = [Bf[:, 10240:10752], Bf[:, 10752:11264]]
        sq_oc = [bcols(22528, 1024), bcols(23552, 1024)]
        xst = [Cf[:, 0:2048], Cf[:, 2048:4096]]
        xbt = [C.ap()[:, 8192:10240], C.ap()[:, 10240:12288]]
        sqt = [C.ap()[:, 12288:14336], C.ap()[:, 14336:16384]]
        attn_fm = C.ap()[:, 0:16384]
        abs_t = [C.ap()[:, 1024 * k:1024 * (k + 1)] for k in range(4)]
        sp_m1 = C.ap()[:, 4096:5120]
        sp_m2 = C.ap()[:, 5120:6144]
        sp_n1 = C.ap()[:, 6144:7168]
        sp_n2 = C.ap()[:, 7168:8192]
        sp_thr = C.ap()[:, 8192:9216]
        sil = [C.ap()[:, 9216:9728], C.ap()[:, 9728:10240]]
        down_fm = C.ap()[:, 0:16384]
        strip = [Df[:, 0:2048], Df[:, 2048:4096]]
        trig_v = Dd.ap()[:, 8192:12288]
        mask_v = Df[:, 6144:6400]
        negmax = [Df[:, 6400:6401], Df[:, 6401:6402]]
        nscale = [Df[:, 6402:6403], Df[:, 6403:6404]]
        sums = [Df[:, 6404:6405], Df[:, 6405:6406]]
        recs = [Df[:, 6406:6407], Df[:, 6407:6408]]
        lrows = [Df[0:1, 6408:6920], Df[0:1, 6920:7432]]
        ln2 = Dd.ap()[:, 0:16384]
        ln2_k = ln2.rearrange("p (kc t) -> p kc t", kc=NKC)
        rows4_m = Df[0:1, 0:1024]
        rows4_r = Df[0:1, 1024:2048]
        x2st = [Df[:, 2048:3072], Df[:, 3072:4096]]
        t4s = [Df[:, 4096:4608], Df[:, 4608:5120]]
        outst = [Df[:, 5120:6144], Df[:, 6144:7168]]
        sq4 = [Dd.ap()[:, 14336:15360], Dd.ap()[:, 15360:16384]]
        ones4c = Dd.ap()[:, 14080:14082][:, 0:1]
        ones4r = Df[0:1, 7040:7168]

        bank_free = [(VE, 0)] * 8
        bank_full = [0] * 8
        slab_free = [(PE, 0)] * 3
        slab_full = [0] * 3
        slab_idx = [0]

        def load_slab(g, src):
            s = slab_idx[0] % 3
            _w(g, slab_free[s])
            d = g.dma_start(out=SLAB[s][:, :], in_=src)
            slab_full[s] = _bump(d, WLD, 16)
            slab_idx[0] += 1
            return s

        def mm_bank():
            b = mm_bank.i % 4
            mm_bank.i += 1
            return b
        mm_bank.i = 0

        def tr_bank():
            b = 5 if (tr_bank.i % 2 == 0) else 6
            tr_bank.i += 1
            return b
        tr_bank.i = 0

        def wslab(oc_key, idx):
            return ap[oc_key][idx].rearrange("p kc f -> p (kc f)")

        def g_pre(g):
            d = g.dma_start(out=trig_v, in_=ap["trig"])
            _bump(d, WLD, 16)
            d = g.dma_start(out=ident_v, in_=ap["ident"])
            _bump(d, WLD, 16)
            g.memset(onescol, 1.0).then_inc(PL.h, 1)
            g.memset(onesrow, 1.0).then_inc(PL.h, 1)
            PL.n += 2
            for key in ("wq", "wk", "wv"):
                load_slab(g, wslab(key, 0))
        block.gpsimd(g_pre)
        ones_ready = PL.n
        tables_ld = 32

        def s_pre(sp):
            _bump(sp.dma_start(out=bias_v, in_=ap["bias"]), LD, 16)
            _bump(sp.dma_start(out=mask_v, in_=ap["mask"]), LD, 16)
        block.sync(s_pre)
        consts_ld = LD.n

        xfree = [(VE, 0), (VE, 0)]
        sfree = [(PE, 0), (PE, 0)]
        rowsbank = [4, 7]
        lrows_free = [(PE, 0), (PE, 0)]
        xload = [0] * NTT
        ln1_done = 0

        for tl in range(NTT):
            r = tl % 2

            def s_ld(sp, tl=tl, r=r):
                _w(sp, xfree[r])
                src = ap["xt"][tl].rearrange("p kc tau -> p (kc tau)")
                d = sp.dma_start(out=xst[r], in_=src)
                xload[tl] = _bump(d, LD, 16)
            block.sync(s_ld)

            def v_cast(v, tl=tl, r=r):
                v.wait_ge(LD.h, xload[tl])
                _w(v, sfree[r])
                _bump(v.tensor_copy(xbt[r], xst[r]), VE)
            block.vector(v_cast)
            cast_done = VE.n

            def g_sq(g, tl=tl, r=r):
                g.wait_ge(LD.h, xload[tl])
                _w(g, sfree[r])
                _bump(g.tensor_tensor(sqt[r], xst[r], xst[r], ALU.mult), PL)
            block.gpsimd(g_sq)
            sq_done = PL.n

            rb = rowsbank[r]

            def t_stats(t, r=r, rb=rb, tl=tl, cast_done=cast_done, sq_done=sq_done):
                t.wait_ge(VE.h, cast_done)
                t.wait_ge(PL.h, sq_done)
                if tl < 2:
                    t.wait_ge(PL.h, ones_ready)
                _w(t, bank_free[rb])
                xb3 = xbt[r].rearrange("p (kc tau) -> p kc tau", kc=NKC)
                sq3 = sqt[r].rearrange("p (kc tau) -> p kc tau", kc=NKC)
                ins = None
                for kc in range(NKC):
                    ins = t.matmul(PB[rb][0:1, 0:128], onescol, xb3[:, kc],
                                   start=(kc == 0), stop=False)
                for kc in range(NKC):
                    ins = t.matmul(PB[rb][0:1, 128:256], onescol, sq3[:, kc],
                                   start=False, stop=(kc == NKC - 1))
                bank_full[rb] = _bump(ins, PE)
            block.tensor(t_stats)
            sfree[r] = (PE, PE.n)

            lr = lrows[r]
            m_r, v_r, r_r, mr_r = lr[:, 0:128], lr[:, 128:256], lr[:, 256:384], lr[:, 384:512]

            def v_rows(v, rb=rb, r=r, m_r=m_r, v_r=v_r, mr_r=mr_r):
                v.wait_ge(PE.h, bank_full[rb])
                _w(v, lrows_free[r])
                v.tensor_scalar(m_r, PB[rb][0:1, 0:128], 1.0 / D, None, ALU.mult)
                v.tensor_scalar(v_r, PB[rb][0:1, 128:256], 1.0 / D, None, ALU.mult)
                v.tensor_tensor(mr_r, m_r, m_r, ALU.mult)
                v.tensor_tensor(v_r, v_r, mr_r, ALU.subtract)
                ins = v.tensor_scalar(v_r, v_r, float(EPS), None, ALU.add)
                _bump(ins, VE)
            block.vector(v_rows)
            var_done = VE.n
            bank_free[rb] = (VE, var_done)

            def s_rstd(s, v_r=v_r, r_r=r_r, var_done=var_done):
                s.wait_ge(VE.h, var_done)
                s.activation(v_r, v_r, AF.Ln)
                _bump(s.activation(r_r, v_r, AF.Exp, scale=-0.5), SE)
            block.scalar(s_rstd)
            rstd_done = SE.n

            def v_mr(v, m_r=m_r, r_r=r_r, mr_r=mr_r, rstd_done=rstd_done):
                v.wait_ge(SE.h, rstd_done)
                _bump(v.tensor_tensor(mr_r, m_r, r_r, ALU.mult), VE)
            block.vector(v_mr)
            rows_done = VE.n

            bc = tr_bank()

            def t_bc(t, bc=bc, mr_r=mr_r, r_r=r_r, rows_done=rows_done):
                t.wait_ge(VE.h, rows_done)
                _w(t, bank_free[bc])
                t.matmul(PB[bc][:, 0:128], onesrow, mr_r, start=True, stop=False)
                ins = t.matmul(PB[bc][:, 128:256], onesrow, r_r, start=False, stop=True)
                bank_full[bc] = _bump(ins, PE)
            block.tensor(t_bc)
            lrows_free[r] = (PE, PE.n)

            def v_norm(v, tl=tl, r=r, bc=bc):
                v.wait_ge(PE.h, bank_full[bc])
                x3 = xst[r].rearrange("p (kc tau) -> p kc tau", kc=NKC)
                t3 = t1s[r].rearrange("p (kc tau) -> p kc tau", kc=NKC)
                rbc = PB[bc][:, 128:256].rearrange("p (o tau) -> p o tau", o=1)\
                    .broadcast_to([128, NKC, 128])
                mbc = PB[bc][:, 0:128].rearrange("p (o tau) -> p o tau", o=1)\
                    .broadcast_to([128, NKC, 128])
                _bump(v.tensor_tensor(t3, x3, rbc, ALU.mult), VE)
                xfree[r] = (VE, VE.n)
                ins = v.tensor_tensor(ln1_k[:, :, tl * 128:tl * 128 + 128], t3, mbc,
                                      ALU.subtract)
                _bump(ins, VE)
            block.vector(v_norm)
            bank_free[bc] = (VE, VE.n)
            ln1_done = VE.n

        if "ln1" in dbg or upto == 0:
            _emit_dbg(block, dbg_bufs.get("ln1"), [ln1[:, :]],
                      [LD, WLD, PE, VE, SE, PL], SO)
            if upto == 0:
                return nc

        avbank = [7, 4]
        strip_free = [(SE, 0), (SE, 0)]
        pst_free = [(PE, 0), (PE, 0)]
        rope_done = 0
        attn_evac = 0
        fill_list = ([("q", f) for f in range(2)]
                     + [("k", f) for f in range(4)]
                     + [("v", f) for f in range(4)])
        slotmap = {"q": 0, "k": 1, "v": 2}
        sdefer = []

        def sflush():
            if not sdefer:
                return
            ops = list(sdefer)
            sdefer.clear()

            def sec(s_):
                for fn in ops:
                    fn(s_)
            block.scalar(sec)

        def defer_s(fn, nbump=1):
            SE.n += nbump
            sdefer.append(fn)
            return SE.n

        def emit_fill(hh, fi):
            kind, f = fill_list[fi]
            s = slotmap[kind]
            pp2 = hh % 2

            def t_fill(t, hh=hh, f=f, s=s, fi=fi):
                if hh == 0 and fi == 0:
                    t.wait_ge(VE.h, ln1_done)
                t.wait_ge(WLD.h, slab_full[s])
                b = mm_bank()
                _w(t, bank_free[b])
                ins = None
                for kc in range(NKC):
                    ins = t.matmul(PB[b][:, 0:512],
                                   SLAB[s][:, kc * 128:kc * 128 + 128],
                                   ln1_k[:, kc, f * 512:f * 512 + 512],
                                   start=(kc == 0), stop=(kc == NKC - 1))
                bank_full[b] = _bump(ins, PE)
                t_fill.b = b
            block.tensor(t_fill)
            b = t_fill.b
            bf_val = bank_full[b]

            def evac(s_, hh=hh, pp2=pp2, kind=kind, f=f, b=b, fi=fi, bf_val=bf_val):
                if hh == 0 and fi == 0:
                    s_.wait_ge(LD.h, consts_ld)
                s_.wait_ge(PE.h, bf_val)
                if kind == "q":
                    dst = Qb[pp2][:, f * 512:f * 512 + 512]
                    bcol = bias_v[:, hh:hh + 1]
                elif kind == "k":
                    dst = Kb[pp2][:, f * 512:f * 512 + 512]
                    bcol = bias_v[:, 16 + hh:17 + hh]
                else:
                    dst = Vfb[pp2][:, f * 512:f * 512 + 512]
                    bcol = bias_v[:, 32 + hh:33 + hh]
                s_.activation(dst, PB[b][:, 0:512], AF.Identity,
                              bias=bcol).then_inc(SE.h, 1)
            ev = defer_s(evac)
            bank_free[b] = (SE, ev)
            if fi in (1, 5, 9):
                slab_free[s] = (PE, PE.n)
            return ev

        for fi in range(10):
            ev0 = emit_fill(0, fi)
        sflush()
        evac_pred = {0: ev0}

        for hh in range(NH):
            pp = hh % 2
            qkv_evac = evac_pred[hh]

            def g_slab(g, hh=hh):
                if hh + 1 < NH:
                    for key in ("wq", "wk", "wv"):
                        load_slab(g, wslab(key, hh + 1))
            block.gpsimd(g_slab)

            prev_rope = rope_done

            def v_shuf(v, pp=pp, qkv_evac=qkv_evac, prev_rope=prev_rope, hh=hh):
                v.wait_ge(SE.h, qkv_evac)
                if prev_rope:
                    v.wait_ge(PL.h, prev_rope)
                if hh == 0:
                    v.wait_ge(WLD.h, tables_ld)
                v.stream_shuffle(ropescr[0:32, 0:1024], Qb[pp][0:32, :], SWAP_MASK)
                ins = v.stream_shuffle(ropescr[0:32, 1024:2048], Kb[pp][0:32, 0:1024],
                                       SWAP_MASK)
                _bump(ins, VE)
            block.vector(v_shuf)
            shuf1 = VE.n

            def g_rope(g, pp=pp, shuf1=shuf1):
                g.wait_ge(VE.h, shuf1)
                cc = trig_v[0:32, 0:2048]
                ss = trig_v[0:32, 2048:4096]
                sco = ropescr[0:32, 0:1024]
                g.tensor_tensor(sco, sco, ss[:, 0:1024], ALU.mult)
                g.tensor_tensor(Qb[pp][0:32, :], Qb[pp][0:32, :], cc[:, 0:1024], ALU.mult)
                g.tensor_tensor(Qb[pp][0:32, :], Qb[pp][0:32, :], sco, ALU.add)
                sck = ropescr[0:32, 1024:2048]
                g.tensor_tensor(sck, sck, ss[:, 0:1024], ALU.mult)
                g.tensor_tensor(Kb[pp][0:32, 0:1024], Kb[pp][0:32, 0:1024],
                                cc[:, 0:1024], ALU.mult)
                ins = g.tensor_tensor(Kb[pp][0:32, 0:1024], Kb[pp][0:32, 0:1024],
                                      sck, ALU.add)
                _bump(ins, PL)
            block.gpsimd(g_rope)
            kh1_done = PL.n

            def v_shuf2(v, pp=pp, kh1_done=kh1_done):
                v.wait_ge(PL.h, kh1_done)
                ins = v.stream_shuffle(ropescr[0:32, 0:1024], Kb[pp][0:32, 1024:2048],
                                       SWAP_MASK)
                _bump(ins, VE)
            block.vector(v_shuf2)
            shuf2 = VE.n

            def g_rope2(g, pp=pp, shuf2=shuf2):
                g.wait_ge(VE.h, shuf2)
                cc = trig_v[0:32, 0:2048]
                ss = trig_v[0:32, 2048:4096]
                sck = ropescr[0:32, 0:1024]
                g.tensor_tensor(sck, sck, ss[:, 1024:2048], ALU.mult)
                g.tensor_tensor(Kb[pp][0:32, 1024:2048], Kb[pp][0:32, 1024:2048],
                                cc[:, 1024:2048], ALU.mult)
                ins = g.tensor_tensor(Kb[pp][0:32, 1024:2048], Kb[pp][0:32, 1024:2048],
                                      sck, ALU.add)
                _bump(ins, PL)
            block.gpsimd(g_rope2)
            rope_done = PL.n

            for tp in range(NTT // 2):
                pair = []

                def t_vtr(t, pp=pp, qkv_evac=qkv_evac, hh=hh, tp=tp, pair=pair):
                    if tp == 0:
                        t.wait_ge(SE.h, qkv_evac)
                        if hh == 0:
                            t.wait_ge(WLD.h, tables_ld)
                    for tt in (2 * tp, 2 * tp + 1):
                        b = tr_bank()
                        _w(t, bank_free[b])
                        ins = t.transpose(PB[b].bitcast(BF16)[:, 0:128],
                                          Vfb[pp][:, tt * 128:tt * 128 + 128], ident_v)
                        bank_full[b] = _bump(ins, PE)
                        pair.append((tt, b))
                block.tensor(t_vtr)

                def v_vtr(v, pp=pp, pair=pair):
                    for tt, b in pair:
                        v.wait_ge(PE.h, bank_full[b])
                        ins = v.tensor_copy(Vtb[pp][:, tt * 128:tt * 128 + 128],
                                            PB[b].bitcast(BF16)[:, 0:128])
                        _bump(ins, VE)
                        bank_free[b] = (VE, VE.n)
                block.vector(v_vtr)

            score_banks = [[] for _ in range(NQT)]
            strip_ready = [0] * NQT
            strip_sready = [0] * NQT
            exp_done = [0] * NQT
            norm_done = [0] * NQT
            rd = rope_done

            def stageA(i, hh=hh, pp=pp, rd=rd):
                r = i % 2
                W = (i + 1) * 128

                def t_sc(t, i=i, r=r, W=W):
                    t.wait_ge(PL.h, rd)
                    _w(t, strip_free[r])
                    for span, base in ((0, 0), (1, 1024)):
                        for ob in range(0, W, 512):
                            n = min(512, W - ob)
                            b = mm_bank()
                            _w(t, bank_free[b])
                            ins = t.matmul(PB[b][:, 0:n],
                                           Qb[pp][:, i * 128:i * 128 + 128],
                                           Kb[pp][:, base + ob:base + ob + n],
                                           start=True, stop=True)
                            bank_full[b] = _bump(ins, PE)
                            score_banks[i].append((span, ob, n, b))
                block.tensor(t_sc)

                copies = [(span, ob, n, b, bank_full[b])
                          for span, ob, n, b in score_banks[i]
                          if (n - 128 if (ob + n == W) else n) > 0]

                def s_ev(s_, copies=copies, r=r, W=W):
                    for span, ob, n, b, bfv in copies:
                        w_un = n - 128 if (ob + n == W) else n
                        s_.wait_ge(PE.h, bfv)
                        lo = span * W + ob
                        s_.activation(strip[r][:, lo:lo + w_un],
                                      PB[b][:, 0:w_un], AF.Copy).then_inc(SE.h, 1)
                if copies:
                    strip_sready[i] = defer_s(s_ev, len(copies))
                else:
                    strip_sready[i] = SE.n

                def v_mask(v, i=i, r=r, W=W, hh=hh):
                    if hh == 0 and i == 0:
                        v.wait_ge(LD.h, consts_ld)
                    if strip_sready[i]:
                        v.wait_ge(SE.h, strip_sready[i])
                    for span, ob, n, b in score_banks[i]:
                        if ob + n == W:
                            v.wait_ge(PE.h, bank_full[b])
                            lo = span * W + ob
                            mk = mask_v[:, 0:128] if span == 0 else mask_v[:, 128:256]
                            ins = v.tensor_tensor(strip[r][:, lo + n - 128:lo + n],
                                                  PB[b][:, n - 128:n], mk, ALU.add)
                            _bump(ins, VE)
                    strip_ready[i] = VE.n
                block.vector(v_mask)
                for span, ob, n, b in score_banks[i]:
                    bank_free[b] = ((VE, strip_ready[i]) if (ob + n == W)
                                    else (SE, strip_sready[i]))

            def stageB(i):
                r = i % 2
                W = (i + 1) * 128

                def v_max(v, i=i, r=r, W=W):
                    v.wait_ge(SE.h, strip_sready[i])
                    v.wait_ge(VE.h, strip_ready[i])
                    v.reduce_max(negmax[r], strip[r][:, 0:2 * W], axis=AXX, negate=True)
                    ins = v.tensor_scalar(nscale[r], negmax[r], ISQ, None, ALU.mult)
                    _bump(ins, VE)
                block.vector(v_max)
                mx = VE.n
                pf = pst_free[r]

                def s_exp(s_, i=i, r=r, W=W, mx=mx, pf=pf):
                    s_.wait_ge(VE.h, mx)
                    _w(s_, pf)
                    s_.activation(Pstr[r][:, 0:2 * W], strip[r][:, 0:2 * W],
                                  AF.Exp, bias=nscale[r][:, 0:1], scale=ISQ,
                                  accum_out=sums[r][:, 0:1]).then_inc(SE.h, 1)
                exp_done[i] = defer_s(s_exp)
                strip_free[r] = (SE, exp_done[i])

                def v_nrm(v, i=i, r=r, W=W):
                    v.wait_ge(SE.h, exp_done[i])
                    _bump(v.reciprocal(recs[r], sums[r]), VE)
                    v.wait_ge(VE.h, VE.n)
                    ins = v.tensor_scalar(Pstr[r][:, 0:2 * W], Pstr[r][:, 0:2 * W],
                                          recs[r][:, 0:1], None, ALU.mult)
                    _bump(ins, VE)
                block.vector(v_nrm)
                norm_done[i] = VE.n

            def stageC(i, hh=hh, pp=pp):
                r = i % 2
                nt = i + 1
                for jp in range(nt):
                    pair = []

                    def t_pt(t, i=i, r=r, jp=jp, pair=pair):
                        if jp == 0:
                            t.wait_ge(VE.h, norm_done[i])
                        for j in (2 * jp, 2 * jp + 1):
                            b = tr_bank()
                            _w(t, bank_free[b])
                            ins = t.transpose(PB[b].bitcast(BF16)[:, 0:128],
                                              Pstr[r][:, j * 128:j * 128 + 128], ident_v)
                            bank_full[b] = _bump(ins, PE)
                            pair.append((j, b))
                    block.tensor(t_pt)

                    def v_pt(v, r=r, pair=pair):
                        for j, b in pair:
                            v.wait_ge(PE.h, bank_full[b])
                            ins = v.tensor_copy(PTb[r][:, j * 128:j * 128 + 128],
                                                PB[b].bitcast(BF16)[:, 0:128])
                            _bump(ins, VE)
                            bank_free[b] = (VE, VE.n)
                    block.vector(v_pt)
                pt_ev = VE.n

                ab = avbank[i % 2]

                def t_av(t, i=i, r=r, nt=nt, ab=ab, pt_ev=pt_ev, pp=pp):
                    t.wait_ge(VE.h, pt_ev)
                    _w(t, bank_free[ab])
                    vts = list(range(nt)) + [8 + k for k in range(nt)]
                    ins = None
                    for j, vt in enumerate(vts):
                        ins = t.matmul(PB[ab][:, 0:128],
                                       Vtb[pp][:, vt * 128:vt * 128 + 128],
                                       PTb[r][:, j * 128:j * 128 + 128],
                                       start=(j == 0), stop=(j == 2 * nt - 1))
                    bank_full[ab] = _bump(ins, PE)
                block.tensor(t_av)
                pst_free[r] = (PE, PE.n)

                def v_av(v, i=i, ab=ab, hh=hh):
                    v.wait_ge(PE.h, bank_full[ab])
                    dst = attn_fm[:, hh * 1024 + i * 128:hh * 1024 + i * 128 + 128]
                    ins = v.tensor_copy(dst, PB[ab][:, 0:128])
                    _bump(ins, VE)
                    bank_free[ab] = (VE, VE.n)
                block.vector(v_av)

            if hh == 0 and "qk0" in dbg:
                sflush()
                _emit_dbg(block, dbg_bufs.get("qk0"),
                          [Qb[0][:, :], Kb[0][:, :], Vtb[0][:, :]],
                          [LD, WLD, PE, VE, SE, PL], SO)
                return nc

            inject = list(range(10)) if hh + 1 < NH else []
            stageA(0)
            for i in range(NQT):
                if i + 1 < NQT:
                    stageA(i + 1)
                stageB(i)
                for _ in range(2):
                    if inject:
                        evac_pred[hh + 1] = emit_fill(hh + 1, inject.pop(0))
                stageC(i)
                if hh == 0 and i == 2 and "sm0" in dbg:
                    sflush()
                    _emit_dbg(block, dbg_bufs.get("sm0"),
                              [Dd.ap()[:, 0:8192], Pstr[0][:, :], Pstr[1][:, :],
                               PTb[0][:, :], Dd.ap()[:, 12800:12832]],
                              [LD, WLD, PE, VE, SE, PL], SO)
                    return nc
            sflush()
            attn_evac = VE.n

        if "attn" in dbg or upto == 1:
            _emit_dbg(block, dbg_bufs.get("attn"), [attn_fm],
                      [LD, WLD, PE, VE, SE, PL], SO)
            if upto == 1:
                return nc

        dn_banks = {}
        sqfree = [0, 0]
        dstat_pe = [0] * NKC
        for oc in range(NKC):
            def g_wd(g, oc=oc):
                load_slab(g, wslab("wd", oc))
            block.gpsimd(g_wd)
            s = (slab_idx[0] - 1) % 3

            def t_dn(t, oc=oc, s=s):
                if oc == 0:
                    t.wait_ge(VE.h, attn_evac)
                t.wait_ge(WLD.h, slab_full[s])
                for tb in range(2):
                    b = mm_bank()
                    _w(t, bank_free[b])
                    ins = None
                    for kc in range(NKC):
                        src = attn_fm[:, kc * 1024 + tb * 512:kc * 1024 + tb * 512 + 512]
                        ins = t.matmul(PB[b][:, 0:512],
                                       SLAB[s][:, kc * 128:kc * 128 + 128], src,
                                       start=(kc == 0), stop=(kc == NKC - 1))
                    bank_full[b] = _bump(ins, PE)
                    dn_banks[(oc, tb)] = b
            block.tensor(t_dn)
            slab_free[s] = (PE, PE.n)

            def s_dnev(s_, oc=oc):
                if oc < 2:
                    s_.wait_ge(VE.h, attn_evac)
                for tb in range(2):
                    b = dn_banks[(oc, tb)]
                    s_.wait_ge(PE.h, bank_full[b])
                    dst = dense_fm[:, oc * 1024 + tb * 512:oc * 1024 + tb * 512 + 512]
                    _bump(s_.activation(dst, PB[b][:, 0:512], AF.Identity,
                                        bias=bias_v[:, 48 + oc:49 + oc]), SE)
            block.scalar(s_dnev)
            dnev = SE.n
            for tb in range(2):
                bank_free[dn_banks[(oc, tb)]] = (SE, dnev)

            def g_dsq(g, oc=oc, dnev=dnev):
                g.wait_ge(SE.h, dnev)
                _w(g, (PE, sqfree[oc % 2]))
                do = dense_fm[:, oc * 1024:oc * 1024 + 1024]
                _bump(g.tensor_tensor(sq_oc[oc % 2], do, do, ALU.mult), PL)
            block.gpsimd(g_dsq)
            dsqd = PL.n

            def t_dst(t, oc=oc, dnev=dnev, dsqd=dsqd):
                t.wait_ge(SE.h, dnev)
                t.wait_ge(PL.h, dsqd)
                if oc == 0:
                    for b in range(4, 8):
                        _w(t, bank_free[b])
                do = dense_fm[:, oc * 1024:oc * 1024 + 1024]
                sq = sq_oc[oc % 2]
                ins = None
                for th in range(2):
                    ins = t.matmul(PB[4 + th][0:1, 0:512], onescol,
                                   do[:, th * 512:th * 512 + 512],
                                   start=(oc == 0), stop=(oc == NKC - 1))
                for th in range(2):
                    ins = t.matmul(PB[6 + th][0:1, 0:512], onescol,
                                   sq[:, th * 512:th * 512 + 512],
                                   start=(oc == 0), stop=(oc == NKC - 1))
                _bump(ins, PE)
            block.tensor(t_dst)
            sqfree[oc % 2] = PE.n
            dstat_pe[oc] = PE.n
        dstat_done = PE.n

        def v_drows(v):
            v.wait_ge(PE.h, dstat_done)
            ins = None
            for th in range(2):
                m = rows_m[:, th * 512:th * 512 + 512]
                r_ = rows_r[:, th * 512:th * 512 + 512]
                v.tensor_scalar(m, PB[4 + th][0:1, 0:512], 1.0 / D, None, ALU.mult)
                v.tensor_scalar(r_, PB[6 + th][0:1, 0:512], 1.0 / D, None, ALU.mult)
                t_ = lrows[0][:, 0:512]
                v.tensor_tensor(t_, m, m, ALU.mult)
                v.tensor_tensor(r_, r_, t_, ALU.subtract)
                ins = v.tensor_scalar(r_, r_, float(EPS), None, ALU.add)
            _bump(ins, VE)
        block.vector(v_drows)
        drow_v = VE.n
        for b in range(4, 8):
            bank_free[b] = (VE, drow_v)

        def s_drstd(s):
            s.wait_ge(VE.h, drow_v)
            s.activation(rows_r, rows_r, AF.Ln)
            _bump(s.activation(rows_r, rows_r, AF.Exp, scale=-0.5), SE)
        block.scalar(s_drstd)
        drstd = SE.n

        def v_dmr(v):
            v.wait_ge(SE.h, drstd)
            _bump(v.tensor_tensor(rows_m, rows_m, rows_r, ALU.mult), VE)
        block.vector(v_dmr)
        dmr = VE.n

        def t_dbc(t):
            t.wait_ge(VE.h, dmr)
            ins = None
            for th in range(2):
                _w(t, bank_free[4 + th])
                _w(t, bank_free[6 + th])
                t.matmul(PB[4 + th][:, 0:512], onesrow,
                         rows_m[:, th * 512:th * 512 + 512], start=True, stop=True)
                ins = t.matmul(PB[6 + th][:, 0:512], onesrow,
                               rows_r[:, th * 512:th * 512 + 512], start=True, stop=True)
            _bump(ins, PE)
        block.tensor(t_dbc)
        dbc = PE.n
        for b in range(4, 8):
            bank_full[b] = dbc

        x2w = [0] * NKC
        x2c_prev = [0, 0]
        for oc in range(NKC):
            r = oc % 2

            def s_xo(sp, oc=oc, r=r):
                if oc < 2:
                    sp.wait_ge(VE.h, attn_evac)
                else:
                    sp.wait_ge(PL.h, x2c_prev[r])
                src = ap["xo"][oc].rearrange("p i tau -> p (i tau)")
                d = sp.dma_start(out=xoc_st[r], in_=src)
                x2w[oc] = 0
                s_xo.ld = _bump(d, LD, 16)
            block.sync(s_xo)
            xold = s_xo.ld

            def v_x2(v, oc=oc, r=r, xold=xold):
                v.wait_ge(PE.h, dbc)
                if oc >= 1:
                    v.wait_ge(PL.h, x2c_prev[1 - r])
                do = dense_fm[:, oc * 1024:oc * 1024 + 1024]
                ins = None
                for th in range(2):
                    t_ = t2s[th]
                    v.tensor_tensor(t_, do[:, th * 512:th * 512 + 512],
                                    PB[6 + th][:, 0:512], ALU.mult)
                    ins = v.tensor_tensor(t_, t_, PB[4 + th][:, 0:512], ALU.subtract)
                _bump(ins, VE)
            block.vector(v_x2)
            tv = VE.n

            def g_x2(g, oc=oc, r=r, tv=tv, xold=xold):
                g.wait_ge(VE.h, tv)
                g.wait_ge(LD.h, xold)
                ins = None
                for th in range(2):
                    ins = g.tensor_tensor(xoc_st[r][:, th * 512:th * 512 + 512],
                                          xoc_st[r][:, th * 512:th * 512 + 512],
                                          t2s[th], ALU.add)
                _bump(ins, PL)
            block.gpsimd(g_x2)
            x2c_prev[r] = PL.n
            x2c = PL.n

            def s_x2w(sp, oc=oc, r=r, x2c=x2c):
                sp.wait_ge(PL.h, x2c)
                d = sp.dma_start(out=ap["x2s"][oc], in_=xoc_st[r])
                x2w[oc] = _bump(d, X2S, 16)
            block.sync(s_x2w)
        p2_pool_done = PL.n

        if "x2" in dbg or upto == 2:
            _emit_dbg(block, None, [], [LD, WLD, PE, VE, SE, PL, X2S], SO)
            if upto == 2:
                return nc

        x2ld = [0] * NKC
        v25free = [0, 0]
        s25free = [0, 0]
        t25free = [0, 0]
        for oc in range(NKC):
            r = oc % 2

            def s_x2r(sp, oc=oc, r=r):
                sp.wait_ge(X2S.h, x2w[oc])
                if oc < 2:
                    sp.wait_ge(PL.h, p2_pool_done)
                else:
                    sp.wait_ge(VE.h, v25free[r])
                    sp.wait_ge(PL.h, s25free[r])
                d = sp.dma_start(out=xoc_st[r], in_=ap["x2s"][oc])
                x2ld[oc] = _bump(d, LD, 16)
            block.sync(s_x2r)

            def v_x2c(v, oc=oc, r=r):
                v.wait_ge(LD.h, x2ld[oc])
                v.wait_ge(PE.h, dstat_pe[oc])
                _bump(v.tensor_copy(x2b_k[:, oc, :], xoc_st[r]), VE)
            block.vector(v_x2c)
            v25free[r] = VE.n
            x2cast = VE.n

            def g_sq2(g, oc=oc, r=r):
                g.wait_ge(LD.h, x2ld[oc])
                if oc < 2:
                    g.wait_ge(PE.h, dstat_done)
                else:
                    g.wait_ge(PE.h, t25free[r])
                _bump(g.tensor_tensor(sq_oc[r], xoc_st[r], xoc_st[r], ALU.mult), PL)
            block.gpsimd(g_sq2)
            s25free[r] = PL.n
            sq2 = PL.n

            def t_2st(t, oc=oc, r=r, x2cast=x2cast, sq2=sq2):
                t.wait_ge(VE.h, x2cast)
                t.wait_ge(PL.h, sq2)
                if oc == 0:
                    for b in range(4):
                        _w(t, bank_free[b])
                ins = None
                for th in range(2):
                    ins = t.matmul(PB[th][0:1, 0:512], onescol,
                                   x2b_k[:, oc, th * 512:th * 512 + 512],
                                   start=(oc == 0), stop=(oc == NKC - 1))
                for th in range(2):
                    ins = t.matmul(PB[2 + th][0:1, 0:512], onescol,
                                   sq_oc[r][:, th * 512:th * 512 + 512],
                                   start=(oc == 0), stop=(oc == NKC - 1))
                _bump(ins, PE)
            block.tensor(t_2st)
            t25free[r] = PE.n
        stat2 = PE.n

        def v_2rows(v):
            v.wait_ge(PE.h, stat2)
            ins = None
            for th in range(2):
                m = rows_m[:, th * 512:th * 512 + 512]
                r_ = rows_r[:, th * 512:th * 512 + 512]
                v.tensor_scalar(m, PB[th][0:1, 0:512], 1.0 / D, None, ALU.mult)
                v.tensor_scalar(r_, PB[2 + th][0:1, 0:512], 1.0 / D, None, ALU.mult)
                t_ = lrows[0][:, 0:512]
                v.tensor_tensor(t_, m, m, ALU.mult)
                v.tensor_tensor(r_, r_, t_, ALU.subtract)
                ins = v.tensor_scalar(r_, r_, float(EPS), None, ALU.add)
            _bump(ins, VE)
        block.vector(v_2rows)
        r2v = VE.n
        for b in range(4):
            bank_free[b] = (VE, r2v)

        def s_2rstd(s):
            s.wait_ge(VE.h, r2v)
            s.activation(rows_r, rows_r, AF.Ln)
            _bump(s.activation(rows_r, rows_r, AF.Exp, scale=-0.5), SE)
        block.scalar(s_2rstd)
        r2s = SE.n

        def v_2mr(v):
            v.wait_ge(SE.h, r2s)
            _bump(v.tensor_tensor(rows_m, rows_m, rows_r, ALU.mult), VE)
        block.vector(v_2mr)
        mr2 = VE.n

        def t_2bc(t):
            t.wait_ge(VE.h, mr2)
            ins = None
            for th in range(2):
                _w(t, bank_free[4 + th])
                _w(t, bank_free[6 + th])
                t.matmul(PB[4 + th][:, 0:512], onesrow,
                         rows_m[:, th * 512:th * 512 + 512], start=True, stop=True)
                ins = t.matmul(PB[6 + th][:, 0:512], onesrow,
                               rows_r[:, th * 512:th * 512 + 512], start=True, stop=True)
            _bump(ins, PE)
        block.tensor(t_2bc)
        bc2 = PE.n

        for oc in range(NKC):
            def v_ln2(v, oc=oc):
                v.wait_ge(PE.h, bc2)
                ins = None
                for th in range(2):
                    dst = ln2_k[:, oc, th * 512:th * 512 + 512]
                    v.tensor_tensor(dst, x2b_k[:, oc, th * 512:th * 512 + 512],
                                    PB[6 + th][:, 0:512], ALU.mult)
                    ins = v.tensor_tensor(dst, dst, PB[4 + th][:, 0:512], ALU.subtract)
                _bump(ins, VE)
            block.vector(v_ln2)
        ln2_done = VE.n
        for b in range(4, 8):
            bank_free[b] = (VE, ln2_done)

        if "ln2" in dbg or upto == 3:
            _emit_dbg(block, dbg_bufs.get("ln2"), [ln2],
                      [LD, WLD, PE, VE, SE, PL, X2S], SO)
            if upto == 3:
                return nc

        hlo = A.ap()
        hhi = Bb.ap()

        def hview(ci):
            return (hlo if ci < 32 else hhi)[:, (ci % 32) * 1024:(ci % 32) * 1024 + 1024]

        gu_banks = {}
        sp_done = [0]
        silfree = [0, 0]
        for ci in range(NIC):
            def g_wgu(g, ci=ci):
                load_slab(g, wslab("wg", ci))
                load_slab(g, wslab("wu", ci))
            block.gpsimd(g_wgu)
            sg = (slab_idx[0] - 2) % 3
            su = (slab_idx[0] - 1) % 3

            def t_gu(t, ci=ci, sg=sg, su=su):
                if ci == 0:
                    t.wait_ge(VE.h, ln2_done)
                for tb in range(2):
                    for kind, s in (("g", sg), ("u", su)):
                        t.wait_ge(WLD.h, slab_full[s])
                        b = mm_bank()
                        _w(t, bank_free[b])
                        ins = None
                        for kc in range(NKC):
                            ins = t.matmul(PB[b][:, 0:512],
                                           SLAB[s][:, kc * 128:kc * 128 + 128],
                                           ln2_k[:, kc, tb * 512:tb * 512 + 512],
                                           start=(kc == 0), stop=(kc == NKC - 1))
                        bank_full[b] = _bump(ins, PE)
                        gu_banks[(ci, tb, kind)] = b
            block.tensor(t_gu)
            slab_free[sg] = (PE, PE.n)
            slab_free[su] = (PE, PE.n)
            gu_pe = PE.n

            def s_sil(s, ci=ci):
                for tb in range(2):
                    b = gu_banks[(ci, tb, "g")]
                    s.wait_ge(PE.h, bank_full[b])
                    if ci >= 1:
                        s.wait_ge(VE.h, silfree[tb])
                    _bump(s.activation(sil[tb], PB[b][:, 0:512], AF.Silu), SE)
            block.scalar(s_sil)
            sil_done = SE.n

            def v_h(v, ci=ci, sil_done=sil_done):
                hv = hview(ci)
                if ci == 0:
                    v.wait_ge(PE.h, bc2)
                if ci == 32:
                    v.wait_ge(X2S.h, x2w[NKC - 1])
                ins = None
                for tb in range(2):
                    bg = gu_banks[(ci, tb, "g")]
                    bu = gu_banks[(ci, tb, "u")]
                    v.wait_ge(SE.h, sil_done)
                    v.wait_ge(PE.h, bank_full[bu])
                    ins = v.tensor_tensor(hv[:, tb * 512:tb * 512 + 512],
                                          sil[tb], PB[bu][:, 0:512], ALU.mult)
                    _bump(ins, VE)
                    silfree[tb] = VE.n
                    bank_free[bg] = (SE, sil_done)
                    bank_free[bu] = (VE, VE.n)
            block.vector(v_h)
            h_done = VE.n

            if ci % 4 == 3 and "hraw" not in dbg:
                c4 = ci // 4

                def g_abs(g, c4=c4, h_done=h_done):
                    g.wait_ge(VE.h, h_done)
                    for jj in range(4):
                        hv = hview(4 * c4 + jj)
                        _bump(g.tensor_tensor(abs_t[jj], hv, hv, ALU.mult), PL)
                block.gpsimd(g_abs)
                absd = PL.n

                def v_sp(v, absd=absd):
                    v.wait_ge(PL.h, absd)
                    v.tensor_tensor(sp_m1, abs_t[0], abs_t[1], ALU.max)
                    v.tensor_tensor(sp_n1, abs_t[0], abs_t[1], ALU.min)
                    v.tensor_tensor(sp_m2, abs_t[2], abs_t[3], ALU.max)
                    v.tensor_tensor(sp_n2, abs_t[2], abs_t[3], ALU.min)
                    v.tensor_tensor(sp_m1, sp_m1, sp_m2, ALU.min)
                    v.tensor_tensor(sp_n1, sp_n1, sp_n2, ALU.max)
                    v.tensor_tensor(sp_thr, sp_m1, sp_n1, ALU.max)
                    ins = None
                    for jj in range(4):
                        ins = v.tensor_tensor(abs_t[jj], abs_t[jj], sp_thr, ALU.is_ge)
                    _bump(ins, VE)
                block.vector(v_sp)
                spv = VE.n

                def g_sp(g, c4=c4, spv=spv):
                    g.wait_ge(VE.h, spv)
                    ins = None
                    for jj in range(4):
                        hv = hview(4 * c4 + jj)
                        ins = g.tensor_tensor(hv, hv, abs_t[jj], ALU.mult)
                        _bump(ins, PL)
                block.gpsimd(g_sp)
                sp_done[0] = PL.n
        h_all = PL.n
        gu_all_pe = PE.n

        if "h" in dbg or "hraw" in dbg or upto == 4:
            _emit_dbg(block, dbg_bufs.get("h") or dbg_bufs.get("hraw"),
                      [hlo[:, :], hhi[:, :]],
                      [LD, WLD, PE, VE, SE, PL, X2S], SO)
            if upto == 4:
                return nc

        for p4 in range(4):
            ocs = list(range(4 * p4, 4 * p4 + 4))
            for ci in range(NIC):
                def g_wdn(g, ci=ci):
                    load_slab(g, ap["wdn"][ci])
                block.gpsimd(g_wdn)
                s = (slab_idx[0] - 1) % 3

                def t_dwn(t, p4=p4, ci=ci, s=s, ocs=ocs):
                    if p4 == 0 and ci == 0:
                        t.wait_ge(PL.h, h_all)
                    t.wait_ge(WLD.h, slab_full[s])
                    ins = None
                    for k, oc in enumerate(ocs):
                        for tb in range(2):
                            b = 2 * k + tb
                            if ci == 0:
                                _w(t, bank_free[b])
                            ins = t.matmul(PB[b][:, 0:512],
                                           SLAB[s][:, oc * 128:oc * 128 + 128],
                                           hview(ci)[:, tb * 512:tb * 512 + 512],
                                           start=(ci == 0), stop=(ci == NIC - 1))
                            if ci == NIC - 1:
                                bank_full[b] = _bump(ins, PE)
                    if ci != NIC - 1:
                        _bump(ins, PE)
                block.tensor(t_dwn)
                slab_free[s] = (PE, PE.n)

            def v_dwev(v, p4=p4, ocs=ocs):
                for k, oc in enumerate(ocs):
                    for tb in range(2):
                        b = 2 * k + tb
                        v.wait_ge(PE.h, bank_full[b])
                        if p4 == 0 and k == 0 and tb == 0:
                            v.wait_ge(PL.h, sp_done[0])
                        dst = down_fm[:, oc * 1024 + tb * 512:oc * 1024 + tb * 512 + 512]
                        _bump(v.tensor_copy(dst, PB[b][:, 0:512]), VE)
                        bank_free[b] = (VE, VE.n)
            block.vector(v_dwev)
        down_done = VE.n

        def g_ones4(g):
            g.wait_ge(VE.h, down_done)
            g.memset(ones4c, 1.0)
            ins = g.memset(ones4r, 1.0)
            _bump(ins, PL)
        block.gpsimd(g_ones4)
        ones4 = PL.n

        sq4free = [0, 0]
        for oc in range(NKC):
            r = oc % 2

            def g_sq4(g, oc=oc, r=r):
                g.wait_ge(VE.h, down_done)
                if oc >= 2:
                    g.wait_ge(PE.h, sq4free[r])
                do = down_fm[:, oc * 1024:oc * 1024 + 1024]
                _bump(g.tensor_tensor(sq4[r], do, do, ALU.mult), PL)
            block.gpsimd(g_sq4)
            sqd = PL.n

            def t_4st(t, oc=oc, r=r, sqd=sqd):
                t.wait_ge(PL.h, sqd)
                if oc == 0:
                    t.wait_ge(PL.h, ones4)
                    for b in range(4, 8):
                        _w(t, bank_free[b])
                do = down_fm[:, oc * 1024:oc * 1024 + 1024]
                ins = None
                for th in range(2):
                    ins = t.matmul(PB[4 + th][0:1, 0:512], ones4c,
                                   do[:, th * 512:th * 512 + 512],
                                   start=(oc == 0), stop=(oc == NKC - 1))
                for th in range(2):
                    ins = t.matmul(PB[6 + th][0:1, 0:512], ones4c,
                                   sq4[r][:, th * 512:th * 512 + 512],
                                   start=(oc == 0), stop=(oc == NKC - 1))
                _bump(ins, PE)
            block.tensor(t_4st)
            sq4free[r] = PE.n
        stat4 = PE.n

        def v_4rows(v):
            v.wait_ge(PE.h, stat4)
            ins = None
            for th in range(2):
                m = rows4_m[:, th * 512:th * 512 + 512]
                r_ = rows4_r[:, th * 512:th * 512 + 512]
                v.tensor_scalar(m, PB[4 + th][0:1, 0:512], 1.0 / D, None, ALU.mult)
                v.tensor_scalar(r_, PB[6 + th][0:1, 0:512], 1.0 / D, None, ALU.mult)
                t_ = t4s[0][0:1, 0:512]
                v.tensor_tensor(t_, m, m, ALU.mult)
                v.tensor_tensor(r_, r_, t_, ALU.subtract)
                ins = v.tensor_scalar(r_, r_, float(EPS), None, ALU.add)
            _bump(ins, VE)
        block.vector(v_4rows)
        r4v = VE.n

        def s_4rstd(s):
            s.wait_ge(VE.h, r4v)
            s.activation(rows4_r, rows4_r, AF.Ln)
            _bump(s.activation(rows4_r, rows4_r, AF.Exp, scale=-0.5), SE)
        block.scalar(s_4rstd)
        r4s = SE.n

        def v_4mr(v):
            v.wait_ge(SE.h, r4s)
            _bump(v.tensor_tensor(rows4_m, rows4_m, rows4_r, ALU.mult), VE)
        block.vector(v_4mr)
        mr4 = VE.n

        def t_4bc(t):
            t.wait_ge(VE.h, mr4)
            ins = None
            for th in range(2):
                _w(t, bank_free[th])
                _w(t, bank_free[2 + th])
                t.matmul(PB[th][:, 0:512], ones4r,
                         rows4_m[:, th * 512:th * 512 + 512], start=True, stop=True)
                ins = t.matmul(PB[2 + th][:, 0:512], ones4r,
                               rows4_r[:, th * 512:th * 512 + 512], start=True, stop=True)
            _bump(ins, PE)
        block.tensor(t_4bc)
        bc4 = PE.n

        x2ld4 = [0] * NKC
        outw = [0] * NKC
        fg_prev = [0, 0]
        for oc in range(NKC):
            r = oc % 2

            def s_x2r4(sp, oc=oc, r=r):
                sp.wait_ge(X2S.h, x2w[oc])
                if oc < 2:
                    sp.wait_ge(VE.h, down_done)
                else:
                    sp.wait_ge(PL.h, fg_prev[r])
                d = sp.dma_start(out=x2st[r], in_=ap["x2s"][oc])
                x2ld4[oc] = _bump(d, LD, 16)
            block.sync(s_x2r4)

            def v_fin(v, oc=oc, r=r):
                v.wait_ge(PE.h, bc4)
                if oc >= 1:
                    v.wait_ge(PL.h, fg_prev[1 - r])
                if oc >= 2:
                    v.wait_ge(SO.h, outw[oc - 2])
                do = down_fm[:, oc * 1024:oc * 1024 + 1024]
                ins = None
                for th in range(2):
                    t_ = t4s[th]
                    v.tensor_tensor(t_, do[:, th * 512:th * 512 + 512],
                                    PB[2 + th][:, 0:512], ALU.mult)
                    ins = v.tensor_tensor(outst[r][:, th * 512:th * 512 + 512],
                                          t_, PB[th][:, 0:512], ALU.subtract)
                _bump(ins, VE)
            block.vector(v_fin)
            fv = VE.n

            def g_fin(g, oc=oc, r=r, fv=fv):
                g.wait_ge(VE.h, fv)
                g.wait_ge(LD.h, x2ld4[oc])
                _bump(g.tensor_tensor(outst[r], outst[r], x2st[r], ALU.add), PL)
            block.gpsimd(g_fin)
            fg_prev[r] = PL.n
            fg = PL.n

            def s_outw(sp, oc=oc, r=r, fg=fg):
                sp.wait_ge(PL.h, fg)
                dst = ap["out"][oc].rearrange("p i tau -> p (i tau)")
                d = sp.dma_start(out=dst, in_=outst[r])
                outw[oc] = _bump(d, SO, 16)
            block.sync(s_outw)

        final_so = SO.n

        def s_fini(sp):
            sp.wait_ge(SO.h, final_so)
        block.sync(s_fini)

    return nc


def _emit_dbg(block, dbuf, srcs, sems, SO):
    vals = [(s, s.n) for s in sems]

    def s_dbg(sp):
        for s, v in vals:
            if v > 0:
                sp.wait_ge(s.h, v)
        off = 0
        if dbuf is not None:
            for src in srcs:
                w = src.shape[-1]
                _bump(sp.dma_start(out=dbuf[:, off:off + w], in_=src), SO, 16)
                off += w
        if SO.n > 0:
            sp.wait_ge(SO.h, SO.n)
    block.sync(s_dbg)


_ENG_ORDER = {"EngineType.SP": 0, "EngineType.Activation": 1, "EngineType.DVE": 2,
              "EngineType.PE": 3, "EngineType.Pool": 4}


REORDER = [0]


def _build(upto=9, dbg=()):
    nc = _build_inner(upto, dbg)
    if not REORDER[0]:
        return nc
    fn = nc.main_func
    blocks = list(fn.blocks)
    first, last = blocks[0], blocks[-1]
    mid = blocks[1:-1]

    def key(b):
        for i in b.instructions:
            e = str(i.engine)
            if e in _ENG_ORDER:
                return _ENG_ORDER[e]
        return 9
    out = [first]
    W = 120
    for w in range(0, len(mid), W):
        out.extend(sorted(mid[w:w + W], key=key))
    out.append(last)
    fn.blocks = out
    return nc



_NC_CACHE = {}


def _get_nc(upto=9, dbg=()):
    key = (upto, tuple(dbg))
    if key not in _NC_CACHE:
        _NC_CACHE[key] = _build(upto, dbg)
    return _NC_CACHE[key]


def _make_in_maps(x, Wqkv, bqkv, Wd, bd, Wg, Wu, Wdn):
    shared = _prep_shared(Wqkv, bqkv, Wd, bd, Wg, Wu, Wdn)
    wkeys = ["wq", "wk", "wv", "wd", "wg", "wu", "wdn", "bias", "ident"]
    in_maps = []
    for c in range(NCORE):
        b, h = c // 2, c % 2
        core = _prep_core(x[b], h, shared)
        m = {k: shared[k] for k in wkeys}
        m.update({k: core[k] for k in ("xt", "xo", "trig", "mask")})
        in_maps.append(m)
    return in_maps


def kernel(L0_attn_hidden_states, L0_attn_W_qkv, L0_attn_b_qkv, L0_attn_W_dense,
           L0_attn_b_dense, L0_ffn_w_gate, L0_ffn_w_up, L0_ffn_w_down,
           _upto=9, _dbg=()):
    x = np.asarray(L0_attn_hidden_states, dtype=np.float32)
    in_maps = _make_in_maps(
        x,
        np.asarray(L0_attn_W_qkv, np.float32), np.asarray(L0_attn_b_qkv, np.float32),
        np.asarray(L0_attn_W_dense, np.float32), np.asarray(L0_attn_b_dense, np.float32),
        np.asarray(L0_ffn_w_gate, np.float32), np.asarray(L0_ffn_w_up, np.float32),
        np.asarray(L0_ffn_w_down, np.float32))
    nc = _get_nc(_upto, _dbg)
    res = run_bass_kernel_spmd(nc, in_maps, core_ids=list(range(NCORE)))
    if _upto < 9:
        return res
    out = np.zeros((B, S, D), dtype=np.float32)
    for c in range(NCORE):
        b, h = c // 2, c % 2
        o = np.asarray(res.results[c]["out"])
        for i in range(NQT):
            g = 2 * i + h
            out[b, 128 * g:128 * g + 128, :] = \
                o[:, :, i, :].transpose(2, 0, 1).reshape(128, D)
    return out


# version 06
# speedup vs baseline: 1.1250x; 1.0512x over previous
import math
from contextlib import ExitStack

import numpy as np
import ml_dtypes

import concourse.bass as bass
import concourse.mybir as mybir
from concourse.bass_utils import run_bass_kernel_spmd

F32 = mybir.dt.float32
BF16 = mybir.dt.bfloat16
AF = mybir.ActivationFunctionType
ALU = mybir.AluOpType
AXX = mybir.AxisListType.X

NCORE = 8
B, S, D, NH, DH, II = 4, 2048, 2048, 16, 128, 8192
NKC = D // 128
NIC = II // 128
NTT = S // 128
NQT = 8
TOWN = NQT * 128
ISQ = 1.0 / math.sqrt(DH)
EPS = 1e-5
NEG = -1.0e30
SWAP_MASK = list(range(16, 32)) + list(range(16))



def _local_tile_order(h):
    return [2 * i + h for i in range(NQT)] + [2 * i + (1 - h) for i in range(NQT)]


def _prep_shared(Wqkv, bqkv, Wd, bd, Wg, Wu, Wdn):
    out = {}
    w3 = Wqkv.reshape(D, NH, 3 * DH)

    def tile_w(mat, noc):
        return np.ascontiguousarray(
            mat.reshape(NKC, 128, noc, 128).transpose(2, 1, 0, 3))

    out["wq"] = tile_w(np.ascontiguousarray(w3[:, :, 0:128]).reshape(D, NH * 128), NH)
    out["wk"] = tile_w(np.ascontiguousarray(w3[:, :, 128:256]).reshape(D, NH * 128), NH)
    out["wv"] = tile_w(np.ascontiguousarray(w3[:, :, 256:384]).reshape(D, NH * 128), NH)
    out["wd"] = tile_w(Wd, NKC)
    g = np.arange(NIC * 128)
    c, j = (g // 128) // 4, (g // 128) % 4
    f = g % 128
    ffn_cols = 4 * (128 * c + f) + j
    out["wg"] = tile_w(np.ascontiguousarray(Wg[:, ffn_cols]), NIC)
    out["wu"] = tile_w(np.ascontiguousarray(Wu[:, ffn_cols]), NIC)
    out["wdn"] = np.ascontiguousarray(Wdn[ffn_cols, :].reshape(NIC, 128, D))
    bias = np.zeros((128, 64), dtype=np.float32)
    b3 = bqkv.reshape(NH, 3 * DH)
    bias[:, 0:NH] = b3[:, 0:128].T
    bias[:, 16:32] = b3[:, 128:256].T
    bias[:, 32:48] = b3[:, 256:384].T
    bias[:, 48:64] = bd.reshape(NKC, 128).T
    out["bias"] = np.ascontiguousarray(bias)
    out["ident"] = np.eye(128, dtype=np.float32)
    out["_tri"] = np.where(np.triu(np.ones((128, 128), np.float32), 1) > 0,
                           np.float32(NEG), np.float32(0))
    return out


def _prep_core(xb, h, shared):
    order = _local_tile_order(h)
    xt = np.ascontiguousarray(
        xb.T.reshape(NKC, 128, NTT, 128).transpose(2, 1, 0, 3)[order])
    xo = np.ascontiguousarray(xt[0:NQT].transpose(2, 1, 0, 3))
    tglob = np.concatenate([np.arange(128 * g, 128 * (g + 1)) for g in order]).astype(np.float64)
    freqs = 10000.0 ** (-np.arange(16, dtype=np.float64) / 16.0)
    ang = tglob[None, :] * freqs[:, None]
    cc = np.cos(ang).astype(np.float32)
    ss = np.sin(ang).astype(np.float32)
    trig = np.zeros((128, 2 * S), dtype=np.float32)
    trig[0:16, 0:S] = cc
    trig[16:32, 0:S] = cc
    trig[0:16, S:2 * S] = -ss
    trig[16:32, S:2 * S] = ss
    mask = np.zeros((128, 256), dtype=np.float32)
    mask[:, 0:128] = shared["_tri"]
    if h == 0:
        mask[:, 128:256] = NEG
    return {"xt": xt, "xo": xo, "trig": np.ascontiguousarray(trig),
            "mask": np.ascontiguousarray(mask)}



class _Sem:
    __slots__ = ("h", "n")

    def __init__(self, h):
        self.h = h
        self.n = 0


def _bump(instr, sem, k=1):
    instr.then_inc(sem.h, k)
    sem.n += k
    return sem.n


def _w(eng, cond):
    sem, val = cond
    if val > 0:
        eng.wait_ge(sem.h, val)


def _build_inner(upto=9, dbg=()):
    nc = bass.Bass("TRN2", target_bir_lowering=False, dynamic_dma_scratch_size=16384)
    ap = {}
    ap["xt"] = nc.dram_tensor("xt", [NTT, 128, NKC, 128], F32, kind="ExternalInput").ap()
    ap["xo"] = nc.dram_tensor("xo", [NKC, 128, NQT, 128], F32, kind="ExternalInput").ap()
    for k, noc in (("wq", NH), ("wk", NH), ("wv", NH), ("wd", NKC), ("wg", NIC), ("wu", NIC)):
        ap[k] = nc.dram_tensor(k, [noc, 128, NKC, 128], F32, kind="ExternalInput").ap()
    ap["wdn"] = nc.dram_tensor("wdn", [NIC, 128, D], F32, kind="ExternalInput").ap()
    ap["bias"] = nc.dram_tensor("bias", [128, 64], F32, kind="ExternalInput").ap()
    ap["trig"] = nc.dram_tensor("trig", [128, 2 * S], F32, kind="ExternalInput").ap()
    ap["mask"] = nc.dram_tensor("mask", [128, 256], F32, kind="ExternalInput").ap()
    ap["ident"] = nc.dram_tensor("ident", [128, 128], F32, kind="ExternalInput").ap()
    ap["out"] = nc.dram_tensor("out", [NKC, 128, NQT, 128], F32, kind="ExternalOutput").ap()
    x2kind = "ExternalOutput" if "x2" in dbg else "Internal"
    ap["x2s"] = nc.dram_tensor("x2s", [NKC, 128, TOWN], F32, kind=x2kind).ap()
    ap["wdnb"] = nc.dram_tensor("wdnb", [NIC, 128, D], BF16, kind="Internal").ap()
    dbg_bufs = {}
    for name in dbg:
        if name != "x2":
            dbg_bufs[name] = nc.dram_tensor("dbg_" + name, [128, 65536], BF16,
                                            kind="ExternalOutput").ap()

    with ExitStack() as ctx:
        E = ctx.enter_context
        A = E(nc.sbuf_tensor("arena_a", [128, 32768], BF16))
        Bb = E(nc.sbuf_tensor("arena_b", [128, 32768], BF16))
        C = E(nc.sbuf_tensor("arena_c", [128, 16384], BF16))
        Dd = E(nc.sbuf_tensor("arena_d", [128, 16384], BF16))
        SLAB = [E(nc.sbuf_tensor(f"slab{i}", [128, 2048], BF16)) for i in range(3)]
        PB = [E(nc.psum_tensor(f"pb{i}", [128, 512], F32)) for i in range(8)]

        LD = _Sem(E(nc.semaphore("LD")))
        WLD = _Sem(E(nc.semaphore("WLD")))
        X2S = _Sem(E(nc.semaphore("X2S")))
        SO = _Sem(E(nc.semaphore("SO")))
        WCL = _Sem(E(nc.semaphore("WCL")))
        WCS = _Sem(E(nc.semaphore("WCS")))
        S4L = _Sem(E(nc.semaphore("S4L")))
        PE = _Sem(E(nc.semaphore("PE")))
        VE = _Sem(E(nc.semaphore("VE")))
        SE = _Sem(E(nc.semaphore("SE")))
        PL = _Sem(E(nc.semaphore("PL")))
        block = E(nc.Block())

        Bf = Bb.ap().bitcast(F32)
        Cf = C.ap().bitcast(F32)
        Df = Dd.ap().bitcast(F32)
        ln1 = A.ap()
        ln1_k = ln1.rearrange("p (kc t) -> p kc t", kc=NKC)

        def bcols(o, w):
            return Bb.ap()[:, o:o + w]
        Qb = [bcols(0, 1024), bcols(1024, 1024)]
        Kb = [bcols(2048, 2048), bcols(4096, 2048)]
        Vfb = [bcols(6144, 2048), bcols(8192, 2048)]
        Vtb = [bcols(10240, 2048), bcols(12288, 2048)]
        Pstr = [bcols(14336, 2048), bcols(16384, 2048)]
        PTb = [bcols(18432, 2048), bcols(20480, 2048)]
        ropescr = bcols(22528, 2048)
        bias_v = Bf[:, 12288:12352]
        ident_v = bcols(24832, 128)
        onescol = bcols(24960, 2)[:, 0:1]
        onesrow = Bf[0:1, 12544:12672]
        rows_m = Bf[0:1, 13696:14720]
        rows_r = Bf[0:1, 14720:15744]
        t1s = [Bf[:, 0:2048], Bf[:, 2048:4096]]
        dense_fm = Bb.ap()[:, 0:16384]
        x2b = Bb.ap()[:, 0:16384]
        x2b_k = x2b.rearrange("p (kc t) -> p kc t", kc=NKC)
        xoc_st = [Bf[:, 8192:9216], Bf[:, 9216:10240]]
        t2s = [Bf[:, 10240:10752], Bf[:, 10752:11264]]
        sq_oc = [bcols(22528, 1024), bcols(23552, 1024)]
        xst = [Cf[:, 0:2048], Cf[:, 2048:4096]]
        xbt = [C.ap()[:, 8192:10240], C.ap()[:, 10240:12288]]
        sqt = [C.ap()[:, 12288:14336], C.ap()[:, 14336:16384]]
        attn_fm = C.ap()[:, 0:16384]
        abs_t = [C.ap()[:, 1024 * k:1024 * (k + 1)] for k in range(4)]
        sp_m1 = C.ap()[:, 4096:5120]
        sp_m2 = C.ap()[:, 5120:6144]
        sp_n1 = C.ap()[:, 6144:7168]
        sp_n2 = C.ap()[:, 7168:8192]
        sp_thr = C.ap()[:, 8192:9216]
        sil = [C.ap()[:, 9216:9728], C.ap()[:, 9728:10240]]
        wcst = [C.ap()[:, 10240:12288], C.ap()[:, 12288:14336]]
        down_fm = C.ap()[:, 0:16384]
        strip = [Df[:, 0:2048], Df[:, 2048:4096]]
        trig_v = Dd.ap()[:, 8192:12288]
        mask_v = Df[:, 6144:6400]
        negmax = [Df[:, 6400:6401], Df[:, 6401:6402]]
        nscale = [Df[:, 6402:6403], Df[:, 6403:6404]]
        sums = [Df[:, 6404:6405], Df[:, 6405:6406]]
        recs = [Df[:, 6406:6407], Df[:, 6407:6408]]
        lrows = [Df[0:1, 6408:6920], Df[0:1, 6920:7432]]
        ln2 = Dd.ap()[:, 0:16384]
        ln2_k = ln2.rearrange("p (kc t) -> p kc t", kc=NKC)
        rows4_m = Df[0:1, 0:1024]
        rows4_r = Df[0:1, 1024:2048]
        x2st = [Df[:, 2048:3072], Df[:, 3072:4096]]
        t4s = [Df[:, 4096:4608], Df[:, 4608:5120]]
        outst = [Df[:, 5120:6144], Df[:, 6144:7168]]
        sq4 = [Dd.ap()[:, 14336:15360], Dd.ap()[:, 15360:16384]]
        ones4c = Dd.ap()[:, 14080:14082][:, 0:1]
        ones4r = Df[0:1, 7040:7168]

        bank_free = [(VE, 0)] * 8
        bank_full = [0] * 8
        slab_free = [(PE, 0)] * 3
        slab_full = [0] * 3
        slab_idx = [0]

        def load_slab(g, src):
            s = slab_idx[0] % 3
            _w(g, slab_free[s])
            d = g.dma_start(out=SLAB[s][:, :], in_=src)
            slab_full[s] = _bump(d, WLD, 16)
            slab_idx[0] += 1
            return s

        def mm_bank():
            b = mm_bank.i % 4
            mm_bank.i += 1
            return b
        mm_bank.i = 0

        def tr_bank():
            b = 5 if (tr_bank.i % 2 == 0) else 6
            tr_bank.i += 1
            return b
        tr_bank.i = 0

        def wslab(oc_key, idx):
            return ap[oc_key][idx].rearrange("p kc f -> p (kc f)")

        def g_pre(g):
            d = g.dma_start(out=trig_v, in_=ap["trig"])
            _bump(d, WLD, 16)
            d = g.dma_start(out=ident_v, in_=ap["ident"])
            _bump(d, WLD, 16)
            g.memset(onescol, 1.0).then_inc(PL.h, 1)
            g.memset(onesrow, 1.0).then_inc(PL.h, 1)
            PL.n += 2
            for key in ("wq", "wk", "wv"):
                load_slab(g, wslab(key, 0))
        block.gpsimd(g_pre)
        ones_ready = PL.n
        tables_ld = 32

        def s_pre(sp):
            _bump(sp.dma_start(out=bias_v, in_=ap["bias"]), LD, 16)
            _bump(sp.dma_start(out=mask_v, in_=ap["mask"]), LD, 16)
        block.sync(s_pre)
        consts_ld = LD.n

        xfree = [(VE, 0), (VE, 0)]
        sfree = [(PE, 0), (PE, 0)]
        rowsbank = [4, 7]
        lrows_free = [(PE, 0), (PE, 0)]
        xload = [0] * NTT
        ln1_done = 0

        for tl in range(NTT):
            r = tl % 2

            def s_ld(sp, tl=tl, r=r):
                _w(sp, xfree[r])
                src = ap["xt"][tl].rearrange("p kc tau -> p (kc tau)")
                d = sp.dma_start(out=xst[r], in_=src)
                xload[tl] = _bump(d, LD, 16)
            block.sync(s_ld)

            def v_cast(v, tl=tl, r=r):
                v.wait_ge(LD.h, xload[tl])
                _w(v, sfree[r])
                _bump(v.tensor_copy(xbt[r], xst[r]), VE)
            block.vector(v_cast)
            cast_done = VE.n

            def g_sq(g, tl=tl, r=r):
                g.wait_ge(LD.h, xload[tl])
                _w(g, sfree[r])
                _bump(g.tensor_tensor(sqt[r], xst[r], xst[r], ALU.mult), PL)
            block.gpsimd(g_sq)
            sq_done = PL.n

            rb = rowsbank[r]

            def t_stats(t, r=r, rb=rb, tl=tl, cast_done=cast_done, sq_done=sq_done):
                t.wait_ge(VE.h, cast_done)
                t.wait_ge(PL.h, sq_done)
                if tl < 2:
                    t.wait_ge(PL.h, ones_ready)
                _w(t, bank_free[rb])
                xb3 = xbt[r].rearrange("p (kc tau) -> p kc tau", kc=NKC)
                sq3 = sqt[r].rearrange("p (kc tau) -> p kc tau", kc=NKC)
                ins = None
                for kc in range(NKC):
                    ins = t.matmul(PB[rb][0:1, 0:128], onescol, xb3[:, kc],
                                   start=(kc == 0), stop=False)
                for kc in range(NKC):
                    ins = t.matmul(PB[rb][0:1, 128:256], onescol, sq3[:, kc],
                                   start=False, stop=(kc == NKC - 1))
                bank_full[rb] = _bump(ins, PE)
            block.tensor(t_stats)
            sfree[r] = (PE, PE.n)

            lr = lrows[r]
            m_r, v_r, r_r, mr_r = lr[:, 0:128], lr[:, 128:256], lr[:, 256:384], lr[:, 384:512]

            def v_rows(v, rb=rb, r=r, m_r=m_r, v_r=v_r, mr_r=mr_r):
                v.wait_ge(PE.h, bank_full[rb])
                _w(v, lrows_free[r])
                v.tensor_scalar(m_r, PB[rb][0:1, 0:128], 1.0 / D, None, ALU.mult)
                v.tensor_scalar(v_r, PB[rb][0:1, 128:256], 1.0 / D, None, ALU.mult)
                v.tensor_tensor(mr_r, m_r, m_r, ALU.mult)
                v.tensor_tensor(v_r, v_r, mr_r, ALU.subtract)
                ins = v.tensor_scalar(v_r, v_r, float(EPS), None, ALU.add)
                _bump(ins, VE)
            block.vector(v_rows)
            var_done = VE.n
            bank_free[rb] = (VE, var_done)

            def s_rstd(s, v_r=v_r, r_r=r_r, var_done=var_done):
                s.wait_ge(VE.h, var_done)
                s.activation(v_r, v_r, AF.Ln)
                _bump(s.activation(r_r, v_r, AF.Exp, scale=-0.5), SE)
            block.scalar(s_rstd)
            rstd_done = SE.n

            def v_mr(v, m_r=m_r, r_r=r_r, mr_r=mr_r, rstd_done=rstd_done):
                v.wait_ge(SE.h, rstd_done)
                _bump(v.tensor_tensor(mr_r, m_r, r_r, ALU.mult), VE)
            block.vector(v_mr)
            rows_done = VE.n

            bc = tr_bank()

            def t_bc(t, bc=bc, mr_r=mr_r, r_r=r_r, rows_done=rows_done):
                t.wait_ge(VE.h, rows_done)
                _w(t, bank_free[bc])
                t.matmul(PB[bc][:, 0:128], onesrow, mr_r, start=True, stop=False)
                ins = t.matmul(PB[bc][:, 128:256], onesrow, r_r, start=False, stop=True)
                bank_full[bc] = _bump(ins, PE)
            block.tensor(t_bc)
            lrows_free[r] = (PE, PE.n)

            def v_norm(v, tl=tl, r=r, bc=bc):
                v.wait_ge(PE.h, bank_full[bc])
                x3 = xst[r].rearrange("p (kc tau) -> p kc tau", kc=NKC)
                t3 = t1s[r].rearrange("p (kc tau) -> p kc tau", kc=NKC)
                rbc = PB[bc][:, 128:256].rearrange("p (o tau) -> p o tau", o=1)\
                    .broadcast_to([128, NKC, 128])
                mbc = PB[bc][:, 0:128].rearrange("p (o tau) -> p o tau", o=1)\
                    .broadcast_to([128, NKC, 128])
                _bump(v.tensor_tensor(t3, x3, rbc, ALU.mult), VE)
                xfree[r] = (VE, VE.n)
                ins = v.tensor_tensor(ln1_k[:, :, tl * 128:tl * 128 + 128], t3, mbc,
                                      ALU.subtract)
                _bump(ins, VE)
            block.vector(v_norm)
            bank_free[bc] = (VE, VE.n)
            ln1_done = VE.n

        if "ln1" in dbg or upto == 0:
            _emit_dbg(block, dbg_bufs.get("ln1"), [ln1[:, :]],
                      [LD, WLD, PE, VE, SE, PL], SO)
            if upto == 0:
                return nc

        avbank = [7, 4]
        strip_free = [(SE, 0), (SE, 0)]
        pst_free = [(PE, 0), (PE, 0)]
        rope_done = 0
        attn_evac = 0
        fill_list = ([("q", f) for f in range(2)]
                     + [("k", f) for f in range(4)]
                     + [("v", f) for f in range(4)])
        slotmap = {"q": 0, "k": 1, "v": 2}
        sdefer = []

        def sflush():
            if not sdefer:
                return
            ops = list(sdefer)
            sdefer.clear()

            def sec(s_):
                for fn in ops:
                    fn(s_)
            block.scalar(sec)

        def defer_s(fn, nbump=1):
            SE.n += nbump
            sdefer.append(fn)
            return SE.n

        def emit_fill(hh, fi):
            kind, f = fill_list[fi]
            s = slotmap[kind]
            pp2 = hh % 2

            def t_fill(t, hh=hh, f=f, s=s, fi=fi):
                if hh == 0 and fi == 0:
                    t.wait_ge(VE.h, ln1_done)
                t.wait_ge(WLD.h, slab_full[s])
                b = mm_bank()
                _w(t, bank_free[b])
                ins = None
                for kc in range(NKC):
                    ins = t.matmul(PB[b][:, 0:512],
                                   SLAB[s][:, kc * 128:kc * 128 + 128],
                                   ln1_k[:, kc, f * 512:f * 512 + 512],
                                   start=(kc == 0), stop=(kc == NKC - 1))
                bank_full[b] = _bump(ins, PE)
                t_fill.b = b
            block.tensor(t_fill)
            b = t_fill.b
            bf_val = bank_full[b]

            def evac(s_, hh=hh, pp2=pp2, kind=kind, f=f, b=b, fi=fi, bf_val=bf_val):
                if hh == 0 and fi == 0:
                    s_.wait_ge(LD.h, consts_ld)
                s_.wait_ge(PE.h, bf_val)
                if kind == "q":
                    dst = Qb[pp2][:, f * 512:f * 512 + 512]
                    bcol = bias_v[:, hh:hh + 1]
                elif kind == "k":
                    dst = Kb[pp2][:, f * 512:f * 512 + 512]
                    bcol = bias_v[:, 16 + hh:17 + hh]
                else:
                    dst = Vfb[pp2][:, f * 512:f * 512 + 512]
                    bcol = bias_v[:, 32 + hh:33 + hh]
                s_.activation(dst, PB[b][:, 0:512], AF.Identity,
                              bias=bcol).then_inc(SE.h, 1)
            ev = defer_s(evac)
            bank_free[b] = (SE, ev)
            if fi in (1, 5, 9):
                slab_free[s] = (PE, PE.n)
            return ev

        for fi in range(10):
            ev0 = emit_fill(0, fi)
        sflush()
        evac_pred = {0: ev0}

        for hh in range(NH):
            pp = hh % 2
            qkv_evac = evac_pred[hh]

            def g_slab(g, hh=hh):
                if hh + 1 < NH:
                    for key in ("wq", "wk", "wv"):
                        load_slab(g, wslab(key, hh + 1))
            block.gpsimd(g_slab)

            prev_rope = rope_done

            def v_shuf(v, pp=pp, qkv_evac=qkv_evac, prev_rope=prev_rope, hh=hh):
                v.wait_ge(SE.h, qkv_evac)
                if prev_rope:
                    v.wait_ge(PL.h, prev_rope)
                if hh == 0:
                    v.wait_ge(WLD.h, tables_ld)
                v.stream_shuffle(ropescr[0:32, 0:1024], Qb[pp][0:32, :], SWAP_MASK)
                ins = v.stream_shuffle(ropescr[0:32, 1024:2048], Kb[pp][0:32, 0:1024],
                                       SWAP_MASK)
                _bump(ins, VE)
            block.vector(v_shuf)
            shuf1 = VE.n

            def g_rope(g, pp=pp, shuf1=shuf1):
                g.wait_ge(VE.h, shuf1)
                cc = trig_v[0:32, 0:2048]
                ss = trig_v[0:32, 2048:4096]
                sco = ropescr[0:32, 0:1024]
                g.tensor_tensor(sco, sco, ss[:, 0:1024], ALU.mult)
                g.tensor_tensor(Qb[pp][0:32, :], Qb[pp][0:32, :], cc[:, 0:1024], ALU.mult)
                g.tensor_tensor(Qb[pp][0:32, :], Qb[pp][0:32, :], sco, ALU.add)
                sck = ropescr[0:32, 1024:2048]
                g.tensor_tensor(sck, sck, ss[:, 0:1024], ALU.mult)
                g.tensor_tensor(Kb[pp][0:32, 0:1024], Kb[pp][0:32, 0:1024],
                                cc[:, 0:1024], ALU.mult)
                ins = g.tensor_tensor(Kb[pp][0:32, 0:1024], Kb[pp][0:32, 0:1024],
                                      sck, ALU.add)
                _bump(ins, PL)
            block.gpsimd(g_rope)
            kh1_done = PL.n

            def v_shuf2(v, pp=pp, kh1_done=kh1_done):
                v.wait_ge(PL.h, kh1_done)
                ins = v.stream_shuffle(ropescr[0:32, 0:1024], Kb[pp][0:32, 1024:2048],
                                       SWAP_MASK)
                _bump(ins, VE)
            block.vector(v_shuf2)
            shuf2 = VE.n

            def g_rope2(g, pp=pp, shuf2=shuf2):
                g.wait_ge(VE.h, shuf2)
                cc = trig_v[0:32, 0:2048]
                ss = trig_v[0:32, 2048:4096]
                sck = ropescr[0:32, 0:1024]
                g.tensor_tensor(sck, sck, ss[:, 1024:2048], ALU.mult)
                g.tensor_tensor(Kb[pp][0:32, 1024:2048], Kb[pp][0:32, 1024:2048],
                                cc[:, 1024:2048], ALU.mult)
                ins = g.tensor_tensor(Kb[pp][0:32, 1024:2048], Kb[pp][0:32, 1024:2048],
                                      sck, ALU.add)
                _bump(ins, PL)
            block.gpsimd(g_rope2)
            rope_done = PL.n

            for tp in range(NTT // 2):
                pair = []

                def t_vtr(t, pp=pp, qkv_evac=qkv_evac, hh=hh, tp=tp, pair=pair):
                    if tp == 0:
                        t.wait_ge(SE.h, qkv_evac)
                        if hh == 0:
                            t.wait_ge(WLD.h, tables_ld)
                    for tt in (2 * tp, 2 * tp + 1):
                        b = tr_bank()
                        _w(t, bank_free[b])
                        ins = t.transpose(PB[b].bitcast(BF16)[:, 0:128],
                                          Vfb[pp][:, tt * 128:tt * 128 + 128], ident_v)
                        bank_full[b] = _bump(ins, PE)
                        pair.append((tt, b))
                block.tensor(t_vtr)

                def v_vtr(v, pp=pp, pair=pair):
                    for tt, b in pair:
                        v.wait_ge(PE.h, bank_full[b])
                        ins = v.tensor_copy(Vtb[pp][:, tt * 128:tt * 128 + 128],
                                            PB[b].bitcast(BF16)[:, 0:128])
                        _bump(ins, VE)
                        bank_free[b] = (VE, VE.n)
                block.vector(v_vtr)

            score_banks = [[] for _ in range(NQT)]
            strip_ready = [0] * NQT
            strip_sready = [0] * NQT
            exp_done = [0] * NQT
            norm_done = [0] * NQT
            rd = rope_done

            def stageA(i, hh=hh, pp=pp, rd=rd):
                r = i % 2
                W = (i + 1) * 128

                def t_sc(t, i=i, r=r, W=W):
                    t.wait_ge(PL.h, rd)
                    _w(t, strip_free[r])
                    for span, base in ((0, 0), (1, 1024)):
                        for ob in range(0, W, 512):
                            n = min(512, W - ob)
                            b = mm_bank()
                            _w(t, bank_free[b])
                            ins = t.matmul(PB[b][:, 0:n],
                                           Qb[pp][:, i * 128:i * 128 + 128],
                                           Kb[pp][:, base + ob:base + ob + n],
                                           start=True, stop=True)
                            bank_full[b] = _bump(ins, PE)
                            score_banks[i].append((span, ob, n, b))
                block.tensor(t_sc)

                copies = [(span, ob, n, b, bank_full[b])
                          for span, ob, n, b in score_banks[i]
                          if (n - 128 if (ob + n == W) else n) > 0]

                def s_ev(s_, copies=copies, r=r, W=W):
                    for span, ob, n, b, bfv in copies:
                        w_un = n - 128 if (ob + n == W) else n
                        s_.wait_ge(PE.h, bfv)
                        lo = span * W + ob
                        s_.activation(strip[r][:, lo:lo + w_un],
                                      PB[b][:, 0:w_un], AF.Copy).then_inc(SE.h, 1)
                if copies:
                    strip_sready[i] = defer_s(s_ev, len(copies))
                else:
                    strip_sready[i] = SE.n

                def v_mask(v, i=i, r=r, W=W, hh=hh):
                    if hh == 0 and i == 0:
                        v.wait_ge(LD.h, consts_ld)
                    if strip_sready[i]:
                        v.wait_ge(SE.h, strip_sready[i])
                    for span, ob, n, b in score_banks[i]:
                        if ob + n == W:
                            v.wait_ge(PE.h, bank_full[b])
                            lo = span * W + ob
                            mk = mask_v[:, 0:128] if span == 0 else mask_v[:, 128:256]
                            ins = v.tensor_tensor(strip[r][:, lo + n - 128:lo + n],
                                                  PB[b][:, n - 128:n], mk, ALU.add)
                            _bump(ins, VE)
                    strip_ready[i] = VE.n
                block.vector(v_mask)
                for span, ob, n, b in score_banks[i]:
                    bank_free[b] = ((VE, strip_ready[i]) if (ob + n == W)
                                    else (SE, strip_sready[i]))

            def stageB(i):
                r = i % 2
                W = (i + 1) * 128

                def v_max(v, i=i, r=r, W=W):
                    v.wait_ge(SE.h, strip_sready[i])
                    v.wait_ge(VE.h, strip_ready[i])
                    v.reduce_max(negmax[r], strip[r][:, 0:2 * W], axis=AXX, negate=True)
                    ins = v.tensor_scalar(nscale[r], negmax[r], ISQ, None, ALU.mult)
                    _bump(ins, VE)
                block.vector(v_max)
                mx = VE.n
                pf = pst_free[r]

                def s_exp(s_, i=i, r=r, W=W, mx=mx, pf=pf):
                    s_.wait_ge(VE.h, mx)
                    _w(s_, pf)
                    s_.activation(Pstr[r][:, 0:2 * W], strip[r][:, 0:2 * W],
                                  AF.Exp, bias=nscale[r][:, 0:1], scale=ISQ,
                                  accum_out=sums[r][:, 0:1]).then_inc(SE.h, 1)
                exp_done[i] = defer_s(s_exp)
                strip_free[r] = (SE, exp_done[i])

                def v_nrm(v, i=i, r=r, W=W):
                    v.wait_ge(SE.h, exp_done[i])
                    _bump(v.reciprocal(recs[r], sums[r]), VE)
                    v.wait_ge(VE.h, VE.n)
                    ins = v.tensor_scalar(Pstr[r][:, 0:2 * W], Pstr[r][:, 0:2 * W],
                                          recs[r][:, 0:1], None, ALU.mult)
                    _bump(ins, VE)
                block.vector(v_nrm)
                norm_done[i] = VE.n

            def stageC(i, hh=hh, pp=pp):
                r = i % 2
                nt = i + 1
                for jp in range(nt):
                    pair = []

                    def t_pt(t, i=i, r=r, jp=jp, pair=pair):
                        if jp == 0:
                            t.wait_ge(VE.h, norm_done[i])
                        for j in (2 * jp, 2 * jp + 1):
                            b = tr_bank()
                            _w(t, bank_free[b])
                            ins = t.transpose(PB[b].bitcast(BF16)[:, 0:128],
                                              Pstr[r][:, j * 128:j * 128 + 128], ident_v)
                            bank_full[b] = _bump(ins, PE)
                            pair.append((j, b))
                    block.tensor(t_pt)

                    def v_pt(v, r=r, pair=pair):
                        for j, b in pair:
                            v.wait_ge(PE.h, bank_full[b])
                            ins = v.tensor_copy(PTb[r][:, j * 128:j * 128 + 128],
                                                PB[b].bitcast(BF16)[:, 0:128])
                            _bump(ins, VE)
                            bank_free[b] = (VE, VE.n)
                    block.vector(v_pt)
                pt_ev = VE.n

                ab = avbank[i % 2]

                def t_av(t, i=i, r=r, nt=nt, ab=ab, pt_ev=pt_ev, pp=pp):
                    t.wait_ge(VE.h, pt_ev)
                    _w(t, bank_free[ab])
                    vts = list(range(nt)) + [8 + k for k in range(nt)]
                    ins = None
                    for j, vt in enumerate(vts):
                        ins = t.matmul(PB[ab][:, 0:128],
                                       Vtb[pp][:, vt * 128:vt * 128 + 128],
                                       PTb[r][:, j * 128:j * 128 + 128],
                                       start=(j == 0), stop=(j == 2 * nt - 1))
                    bank_full[ab] = _bump(ins, PE)
                block.tensor(t_av)
                pst_free[r] = (PE, PE.n)

                def v_av(v, i=i, ab=ab, hh=hh):
                    v.wait_ge(PE.h, bank_full[ab])
                    dst = attn_fm[:, hh * 1024 + i * 128:hh * 1024 + i * 128 + 128]
                    ins = v.tensor_copy(dst, PB[ab][:, 0:128])
                    _bump(ins, VE)
                    bank_free[ab] = (VE, VE.n)
                block.vector(v_av)

            if hh == 0 and "qk0" in dbg:
                sflush()
                _emit_dbg(block, dbg_bufs.get("qk0"),
                          [Qb[0][:, :], Kb[0][:, :], Vtb[0][:, :]],
                          [LD, WLD, PE, VE, SE, PL], SO)
                return nc

            inject = list(range(10)) if hh + 1 < NH else []
            stageA(0)
            for i in range(NQT):
                if i + 1 < NQT:
                    stageA(i + 1)
                stageB(i)
                for _ in range(2):
                    if inject:
                        evac_pred[hh + 1] = emit_fill(hh + 1, inject.pop(0))
                stageC(i)
                if hh == 0 and i == 2 and "sm0" in dbg:
                    sflush()
                    _emit_dbg(block, dbg_bufs.get("sm0"),
                              [Dd.ap()[:, 0:8192], Pstr[0][:, :], Pstr[1][:, :],
                               PTb[0][:, :], Dd.ap()[:, 12800:12832]],
                              [LD, WLD, PE, VE, SE, PL], SO)
                    return nc
            sflush()
            attn_evac = VE.n

        if "attn" in dbg or upto == 1:
            _emit_dbg(block, dbg_bufs.get("attn"), [attn_fm],
                      [LD, WLD, PE, VE, SE, PL], SO)
            if upto == 1:
                return nc

        dn_banks = {}
        sqfree = [0, 0]
        dstat_pe = [0] * NKC
        for oc in range(NKC):
            def g_wd(g, oc=oc):
                load_slab(g, wslab("wd", oc))
            block.gpsimd(g_wd)
            s = (slab_idx[0] - 1) % 3

            def t_dn(t, oc=oc, s=s):
                if oc == 0:
                    t.wait_ge(VE.h, attn_evac)
                t.wait_ge(WLD.h, slab_full[s])
                for tb in range(2):
                    b = mm_bank()
                    _w(t, bank_free[b])
                    ins = None
                    for kc in range(NKC):
                        src = attn_fm[:, kc * 1024 + tb * 512:kc * 1024 + tb * 512 + 512]
                        ins = t.matmul(PB[b][:, 0:512],
                                       SLAB[s][:, kc * 128:kc * 128 + 128], src,
                                       start=(kc == 0), stop=(kc == NKC - 1))
                    bank_full[b] = _bump(ins, PE)
                    dn_banks[(oc, tb)] = b
            block.tensor(t_dn)
            slab_free[s] = (PE, PE.n)

            def s_dnev(s_, oc=oc):
                if oc < 2:
                    s_.wait_ge(VE.h, attn_evac)
                for tb in range(2):
                    b = dn_banks[(oc, tb)]
                    s_.wait_ge(PE.h, bank_full[b])
                    dst = dense_fm[:, oc * 1024 + tb * 512:oc * 1024 + tb * 512 + 512]
                    _bump(s_.activation(dst, PB[b][:, 0:512], AF.Identity,
                                        bias=bias_v[:, 48 + oc:49 + oc]), SE)
            block.scalar(s_dnev)
            dnev = SE.n
            for tb in range(2):
                bank_free[dn_banks[(oc, tb)]] = (SE, dnev)

            def g_dsq(g, oc=oc, dnev=dnev):
                g.wait_ge(SE.h, dnev)
                _w(g, (PE, sqfree[oc % 2]))
                do = dense_fm[:, oc * 1024:oc * 1024 + 1024]
                _bump(g.tensor_tensor(sq_oc[oc % 2], do, do, ALU.mult), PL)
            block.gpsimd(g_dsq)
            dsqd = PL.n

            def t_dst(t, oc=oc, dnev=dnev, dsqd=dsqd):
                t.wait_ge(SE.h, dnev)
                t.wait_ge(PL.h, dsqd)
                if oc == 0:
                    for b in range(4, 8):
                        _w(t, bank_free[b])
                do = dense_fm[:, oc * 1024:oc * 1024 + 1024]
                sq = sq_oc[oc % 2]
                ins = None
                for th in range(2):
                    ins = t.matmul(PB[4 + th][0:1, 0:512], onescol,
                                   do[:, th * 512:th * 512 + 512],
                                   start=(oc == 0), stop=(oc == NKC - 1))
                for th in range(2):
                    ins = t.matmul(PB[6 + th][0:1, 0:512], onescol,
                                   sq[:, th * 512:th * 512 + 512],
                                   start=(oc == 0), stop=(oc == NKC - 1))
                _bump(ins, PE)
            block.tensor(t_dst)
            sqfree[oc % 2] = PE.n
            dstat_pe[oc] = PE.n
        dstat_done = PE.n

        def v_drows(v):
            v.wait_ge(PE.h, dstat_done)
            ins = None
            for th in range(2):
                m = rows_m[:, th * 512:th * 512 + 512]
                r_ = rows_r[:, th * 512:th * 512 + 512]
                v.tensor_scalar(m, PB[4 + th][0:1, 0:512], 1.0 / D, None, ALU.mult)
                v.tensor_scalar(r_, PB[6 + th][0:1, 0:512], 1.0 / D, None, ALU.mult)
                t_ = lrows[0][:, 0:512]
                v.tensor_tensor(t_, m, m, ALU.mult)
                v.tensor_tensor(r_, r_, t_, ALU.subtract)
                ins = v.tensor_scalar(r_, r_, float(EPS), None, ALU.add)
            _bump(ins, VE)
        block.vector(v_drows)
        drow_v = VE.n
        for b in range(4, 8):
            bank_free[b] = (VE, drow_v)

        def s_drstd(s):
            s.wait_ge(VE.h, drow_v)
            s.activation(rows_r, rows_r, AF.Ln)
            _bump(s.activation(rows_r, rows_r, AF.Exp, scale=-0.5), SE)
        block.scalar(s_drstd)
        drstd = SE.n

        def v_dmr(v):
            v.wait_ge(SE.h, drstd)
            _bump(v.tensor_tensor(rows_m, rows_m, rows_r, ALU.mult), VE)
        block.vector(v_dmr)
        dmr = VE.n

        def t_dbc(t):
            t.wait_ge(VE.h, dmr)
            ins = None
            for th in range(2):
                _w(t, bank_free[4 + th])
                _w(t, bank_free[6 + th])
                t.matmul(PB[4 + th][:, 0:512], onesrow,
                         rows_m[:, th * 512:th * 512 + 512], start=True, stop=True)
                ins = t.matmul(PB[6 + th][:, 0:512], onesrow,
                               rows_r[:, th * 512:th * 512 + 512], start=True, stop=True)
            _bump(ins, PE)
        block.tensor(t_dbc)
        dbc = PE.n
        for b in range(4, 8):
            bank_full[b] = dbc

        x2w = [0] * NKC
        x2c_prev = [0, 0]
        for oc in range(NKC):
            r = oc % 2

            def s_xo(sp, oc=oc, r=r):
                if oc < 2:
                    sp.wait_ge(VE.h, attn_evac)
                else:
                    sp.wait_ge(PL.h, x2c_prev[r])
                src = ap["xo"][oc].rearrange("p i tau -> p (i tau)")
                d = sp.dma_start(out=xoc_st[r], in_=src)
                x2w[oc] = 0
                s_xo.ld = _bump(d, LD, 16)
            block.sync(s_xo)
            xold = s_xo.ld

            def v_x2(v, oc=oc, r=r, xold=xold):
                v.wait_ge(PE.h, dbc)
                if oc >= 1:
                    v.wait_ge(PL.h, x2c_prev[1 - r])
                do = dense_fm[:, oc * 1024:oc * 1024 + 1024]
                ins = None
                for th in range(2):
                    t_ = t2s[th]
                    v.tensor_tensor(t_, do[:, th * 512:th * 512 + 512],
                                    PB[6 + th][:, 0:512], ALU.mult)
                    ins = v.tensor_tensor(t_, t_, PB[4 + th][:, 0:512], ALU.subtract)
                _bump(ins, VE)
            block.vector(v_x2)
            tv = VE.n

            def g_x2(g, oc=oc, r=r, tv=tv, xold=xold):
                g.wait_ge(VE.h, tv)
                g.wait_ge(LD.h, xold)
                ins = None
                for th in range(2):
                    ins = g.tensor_tensor(xoc_st[r][:, th * 512:th * 512 + 512],
                                          xoc_st[r][:, th * 512:th * 512 + 512],
                                          t2s[th], ALU.add)
                _bump(ins, PL)
            block.gpsimd(g_x2)
            x2c_prev[r] = PL.n
            x2c = PL.n

            def s_x2w(sp, oc=oc, r=r, x2c=x2c):
                sp.wait_ge(PL.h, x2c)
                d = sp.dma_start(out=ap["x2s"][oc], in_=xoc_st[r])
                x2w[oc] = _bump(d, X2S, 16)
            block.sync(s_x2w)
        p2_pool_done = PL.n

        if "x2" in dbg or upto == 2:
            _emit_dbg(block, None, [], [LD, WLD, PE, VE, SE, PL, X2S], SO)
            if upto == 2:
                return nc

        x2ld = [0] * NKC
        v25free = [0, 0]
        s25free = [0, 0]
        t25free = [0, 0]
        for oc in range(NKC):
            r = oc % 2

            def s_x2r(sp, oc=oc, r=r):
                sp.wait_ge(X2S.h, x2w[oc])
                if oc < 2:
                    sp.wait_ge(PL.h, p2_pool_done)
                else:
                    sp.wait_ge(VE.h, v25free[r])
                    sp.wait_ge(PL.h, s25free[r])
                d = sp.dma_start(out=xoc_st[r], in_=ap["x2s"][oc])
                x2ld[oc] = _bump(d, LD, 16)
            block.sync(s_x2r)

            def v_x2c(v, oc=oc, r=r):
                v.wait_ge(LD.h, x2ld[oc])
                v.wait_ge(PE.h, dstat_pe[oc])
                _bump(v.tensor_copy(x2b_k[:, oc, :], xoc_st[r]), VE)
            block.vector(v_x2c)
            v25free[r] = VE.n
            x2cast = VE.n

            def g_sq2(g, oc=oc, r=r):
                g.wait_ge(LD.h, x2ld[oc])
                if oc < 2:
                    g.wait_ge(PE.h, dstat_done)
                else:
                    g.wait_ge(PE.h, t25free[r])
                _bump(g.tensor_tensor(sq_oc[r], xoc_st[r], xoc_st[r], ALU.mult), PL)
            block.gpsimd(g_sq2)
            s25free[r] = PL.n
            sq2 = PL.n

            def t_2st(t, oc=oc, r=r, x2cast=x2cast, sq2=sq2):
                t.wait_ge(VE.h, x2cast)
                t.wait_ge(PL.h, sq2)
                if oc == 0:
                    for b in range(4):
                        _w(t, bank_free[b])
                ins = None
                for th in range(2):
                    ins = t.matmul(PB[th][0:1, 0:512], onescol,
                                   x2b_k[:, oc, th * 512:th * 512 + 512],
                                   start=(oc == 0), stop=(oc == NKC - 1))
                for th in range(2):
                    ins = t.matmul(PB[2 + th][0:1, 0:512], onescol,
                                   sq_oc[r][:, th * 512:th * 512 + 512],
                                   start=(oc == 0), stop=(oc == NKC - 1))
                _bump(ins, PE)
            block.tensor(t_2st)
            t25free[r] = PE.n
        stat2 = PE.n

        def v_2rows(v):
            v.wait_ge(PE.h, stat2)
            ins = None
            for th in range(2):
                m = rows_m[:, th * 512:th * 512 + 512]
                r_ = rows_r[:, th * 512:th * 512 + 512]
                v.tensor_scalar(m, PB[th][0:1, 0:512], 1.0 / D, None, ALU.mult)
                v.tensor_scalar(r_, PB[2 + th][0:1, 0:512], 1.0 / D, None, ALU.mult)
                t_ = lrows[0][:, 0:512]
                v.tensor_tensor(t_, m, m, ALU.mult)
                v.tensor_tensor(r_, r_, t_, ALU.subtract)
                ins = v.tensor_scalar(r_, r_, float(EPS), None, ALU.add)
            _bump(ins, VE)
        block.vector(v_2rows)
        r2v = VE.n
        for b in range(4):
            bank_free[b] = (VE, r2v)

        def s_2rstd(s):
            s.wait_ge(VE.h, r2v)
            s.activation(rows_r, rows_r, AF.Ln)
            _bump(s.activation(rows_r, rows_r, AF.Exp, scale=-0.5), SE)
        block.scalar(s_2rstd)
        r2s = SE.n

        def v_2mr(v):
            v.wait_ge(SE.h, r2s)
            _bump(v.tensor_tensor(rows_m, rows_m, rows_r, ALU.mult), VE)
        block.vector(v_2mr)
        mr2 = VE.n

        def t_2bc(t):
            t.wait_ge(VE.h, mr2)
            ins = None
            for th in range(2):
                _w(t, bank_free[4 + th])
                _w(t, bank_free[6 + th])
                t.matmul(PB[4 + th][:, 0:512], onesrow,
                         rows_m[:, th * 512:th * 512 + 512], start=True, stop=True)
                ins = t.matmul(PB[6 + th][:, 0:512], onesrow,
                               rows_r[:, th * 512:th * 512 + 512], start=True, stop=True)
            _bump(ins, PE)
        block.tensor(t_2bc)
        bc2 = PE.n

        for oc in range(NKC):
            def v_ln2(v, oc=oc):
                v.wait_ge(PE.h, bc2)
                ins = None
                for th in range(2):
                    dst = ln2_k[:, oc, th * 512:th * 512 + 512]
                    v.tensor_tensor(dst, x2b_k[:, oc, th * 512:th * 512 + 512],
                                    PB[6 + th][:, 0:512], ALU.mult)
                    ins = v.tensor_tensor(dst, dst, PB[4 + th][:, 0:512], ALU.subtract)
                _bump(ins, VE)
            block.vector(v_ln2)
        ln2_done = VE.n
        for b in range(4, 8):
            bank_free[b] = (VE, ln2_done)

        if "ln2" in dbg or upto == 3:
            _emit_dbg(block, dbg_bufs.get("ln2"), [ln2],
                      [LD, WLD, PE, VE, SE, PL, X2S], SO)
            if upto == 3:
                return nc

        hlo = A.ap()
        hhi = Bb.ap()

        def hview(ci):
            return (hlo if ci < 32 else hhi)[:, (ci % 32) * 1024:(ci % 32) * 1024 + 1024]

        gu_banks = {}
        sp_done = [0]
        silfree = [0, 0]
        wcl = [0] * NIC
        wcs = [0] * NIC
        for ci in range(NIC):
            def g_wgu(g, ci=ci):
                load_slab(g, wslab("wg", ci))
                load_slab(g, wslab("wu", ci))
                r3 = ci % 2
                if ci < 2:
                    g.wait_ge(PE.h, bc2)
                else:
                    g.wait_ge(WCS.h, wcs[ci - 2])
                d = g.dma_start(out=wcst[r3], in_=ap["wdn"][ci])
                wcl[ci] = _bump(d, WCL, 16)
            block.gpsimd(g_wgu)

            def s_wc(sp, ci=ci):
                sp.wait_ge(WCL.h, wcl[ci])
                d = sp.dma_start(out=ap["wdnb"][ci], in_=wcst[ci % 2])
                wcs[ci] = _bump(d, WCS, 16)
            block.sync(s_wc)
            sg = (slab_idx[0] - 2) % 3
            su = (slab_idx[0] - 1) % 3

            def t_gu(t, ci=ci, sg=sg, su=su):
                if ci == 0:
                    t.wait_ge(VE.h, ln2_done)
                for tb in range(2):
                    for kind, s in (("g", sg), ("u", su)):
                        t.wait_ge(WLD.h, slab_full[s])
                        b = mm_bank()
                        _w(t, bank_free[b])
                        ins = None
                        for kc in range(NKC):
                            ins = t.matmul(PB[b][:, 0:512],
                                           SLAB[s][:, kc * 128:kc * 128 + 128],
                                           ln2_k[:, kc, tb * 512:tb * 512 + 512],
                                           start=(kc == 0), stop=(kc == NKC - 1))
                        bank_full[b] = _bump(ins, PE)
                        gu_banks[(ci, tb, kind)] = b
            block.tensor(t_gu)
            slab_free[sg] = (PE, PE.n)
            slab_free[su] = (PE, PE.n)
            gu_pe = PE.n

            def s_sil(s, ci=ci):
                for tb in range(2):
                    b = gu_banks[(ci, tb, "g")]
                    s.wait_ge(PE.h, bank_full[b])
                    if ci >= 1:
                        s.wait_ge(VE.h, silfree[tb])
                    _bump(s.activation(sil[tb], PB[b][:, 0:512], AF.Silu), SE)
            block.scalar(s_sil)
            sil_done = SE.n

            def v_h(v, ci=ci, sil_done=sil_done):
                hv = hview(ci)
                if ci == 0:
                    v.wait_ge(PE.h, bc2)
                if ci == 32:
                    v.wait_ge(X2S.h, x2w[NKC - 1])
                ins = None
                for tb in range(2):
                    bg = gu_banks[(ci, tb, "g")]
                    bu = gu_banks[(ci, tb, "u")]
                    v.wait_ge(SE.h, sil_done)
                    v.wait_ge(PE.h, bank_full[bu])
                    ins = v.tensor_tensor(hv[:, tb * 512:tb * 512 + 512],
                                          sil[tb], PB[bu][:, 0:512], ALU.mult)
                    _bump(ins, VE)
                    silfree[tb] = VE.n
                    bank_free[bg] = (SE, sil_done)
                    bank_free[bu] = (VE, VE.n)
            block.vector(v_h)
            h_done = VE.n

            if ci % 4 == 3 and "hraw" not in dbg:
                c4 = ci // 4

                def g_abs(g, c4=c4, h_done=h_done):
                    g.wait_ge(VE.h, h_done)
                    for jj in range(4):
                        hv = hview(4 * c4 + jj)
                        _bump(g.tensor_tensor(abs_t[jj], hv, hv, ALU.mult), PL)
                block.gpsimd(g_abs)
                absd = PL.n

                def v_sp(v, absd=absd):
                    v.wait_ge(PL.h, absd)
                    v.tensor_tensor(sp_m1, abs_t[0], abs_t[1], ALU.max)
                    v.tensor_tensor(sp_n1, abs_t[0], abs_t[1], ALU.min)
                    v.tensor_tensor(sp_m2, abs_t[2], abs_t[3], ALU.max)
                    v.tensor_tensor(sp_n2, abs_t[2], abs_t[3], ALU.min)
                    v.tensor_tensor(sp_m1, sp_m1, sp_m2, ALU.min)
                    v.tensor_tensor(sp_n1, sp_n1, sp_n2, ALU.max)
                    v.tensor_tensor(sp_thr, sp_m1, sp_n1, ALU.max)
                    ins = None
                    for jj in range(4):
                        ins = v.tensor_tensor(abs_t[jj], abs_t[jj], sp_thr, ALU.is_ge)
                    _bump(ins, VE)
                block.vector(v_sp)
                spv = VE.n

                def g_sp(g, c4=c4, spv=spv):
                    g.wait_ge(VE.h, spv)
                    ins = None
                    for jj in range(4):
                        hv = hview(4 * c4 + jj)
                        ins = g.tensor_tensor(hv, hv, abs_t[jj], ALU.mult)
                        _bump(ins, PL)
                block.gpsimd(g_sp)
                sp_done[0] = PL.n
        h_all = PL.n
        gu_all_pe = PE.n

        if "h" in dbg or "hraw" in dbg or upto == 4:
            _emit_dbg(block, dbg_bufs.get("h") or dbg_bufs.get("hraw"),
                      [hlo[:, :], hhi[:, :]],
                      [LD, WLD, PE, VE, SE, PL, X2S], SO)
            if upto == 4:
                return nc

        wcs_all = WCS.n
        for p4 in range(4):
            ocs = list(range(4 * p4, 4 * p4 + 4))
            for ci in range(NIC):
                def s_wdn(sp, p4=p4, ci=ci):
                    if p4 == 0 and ci == 0:
                        sp.wait_ge(WCS.h, wcs_all)
                    s = slab_idx[0] % 3
                    _w(sp, slab_free[s])
                    d = sp.dma_start(out=SLAB[s][:, :], in_=ap["wdnb"][ci])
                    slab_full[s] = _bump(d, S4L, 16)
                    slab_idx[0] += 1
                block.sync(s_wdn)
                s = (slab_idx[0] - 1) % 3

                def t_dwn(t, p4=p4, ci=ci, s=s, ocs=ocs):
                    if p4 == 0 and ci == 0:
                        t.wait_ge(PL.h, h_all)
                    t.wait_ge(S4L.h, slab_full[s])
                    ins = None
                    for k, oc in enumerate(ocs):
                        for tb in range(2):
                            b = 2 * k + tb
                            if ci == 0:
                                _w(t, bank_free[b])
                            ins = t.matmul(PB[b][:, 0:512],
                                           SLAB[s][:, oc * 128:oc * 128 + 128],
                                           hview(ci)[:, tb * 512:tb * 512 + 512],
                                           start=(ci == 0), stop=(ci == NIC - 1))
                            if ci == NIC - 1:
                                bank_full[b] = _bump(ins, PE)
                    if ci != NIC - 1:
                        _bump(ins, PE)
                block.tensor(t_dwn)
                slab_free[s] = (PE, PE.n)

            def v_dwev(v, p4=p4, ocs=ocs):
                for k, oc in enumerate(ocs):
                    for tb in range(2):
                        b = 2 * k + tb
                        v.wait_ge(PE.h, bank_full[b])
                        if p4 == 0 and k == 0 and tb == 0:
                            v.wait_ge(PL.h, sp_done[0])
                        if p4 == 2 and k == 0 and tb == 0:
                            v.wait_ge(WCS.h, wcs_all)
                        dst = down_fm[:, oc * 1024 + tb * 512:oc * 1024 + tb * 512 + 512]
                        _bump(v.tensor_copy(dst, PB[b][:, 0:512]), VE)
                        bank_free[b] = (VE, VE.n)
            block.vector(v_dwev)
        down_done = VE.n

        def g_ones4(g):
            g.wait_ge(VE.h, down_done)
            g.memset(ones4c, 1.0)
            ins = g.memset(ones4r, 1.0)
            _bump(ins, PL)
        block.gpsimd(g_ones4)
        ones4 = PL.n

        sq4free = [0, 0]
        for oc in range(NKC):
            r = oc % 2

            def g_sq4(g, oc=oc, r=r):
                g.wait_ge(VE.h, down_done)
                if oc >= 2:
                    g.wait_ge(PE.h, sq4free[r])
                do = down_fm[:, oc * 1024:oc * 1024 + 1024]
                _bump(g.tensor_tensor(sq4[r], do, do, ALU.mult), PL)
            block.gpsimd(g_sq4)
            sqd = PL.n

            def t_4st(t, oc=oc, r=r, sqd=sqd):
                t.wait_ge(PL.h, sqd)
                if oc == 0:
                    t.wait_ge(PL.h, ones4)
                    for b in range(4, 8):
                        _w(t, bank_free[b])
                do = down_fm[:, oc * 1024:oc * 1024 + 1024]
                ins = None
                for th in range(2):
                    ins = t.matmul(PB[4 + th][0:1, 0:512], ones4c,
                                   do[:, th * 512:th * 512 + 512],
                                   start=(oc == 0), stop=(oc == NKC - 1))
                for th in range(2):
                    ins = t.matmul(PB[6 + th][0:1, 0:512], ones4c,
                                   sq4[r][:, th * 512:th * 512 + 512],
                                   start=(oc == 0), stop=(oc == NKC - 1))
                _bump(ins, PE)
            block.tensor(t_4st)
            sq4free[r] = PE.n
        stat4 = PE.n

        def v_4rows(v):
            v.wait_ge(PE.h, stat4)
            ins = None
            for th in range(2):
                m = rows4_m[:, th * 512:th * 512 + 512]
                r_ = rows4_r[:, th * 512:th * 512 + 512]
                v.tensor_scalar(m, PB[4 + th][0:1, 0:512], 1.0 / D, None, ALU.mult)
                v.tensor_scalar(r_, PB[6 + th][0:1, 0:512], 1.0 / D, None, ALU.mult)
                t_ = t4s[0][0:1, 0:512]
                v.tensor_tensor(t_, m, m, ALU.mult)
                v.tensor_tensor(r_, r_, t_, ALU.subtract)
                ins = v.tensor_scalar(r_, r_, float(EPS), None, ALU.add)
            _bump(ins, VE)
        block.vector(v_4rows)
        r4v = VE.n

        def s_4rstd(s):
            s.wait_ge(VE.h, r4v)
            s.activation(rows4_r, rows4_r, AF.Ln)
            _bump(s.activation(rows4_r, rows4_r, AF.Exp, scale=-0.5), SE)
        block.scalar(s_4rstd)
        r4s = SE.n

        def v_4mr(v):
            v.wait_ge(SE.h, r4s)
            _bump(v.tensor_tensor(rows4_m, rows4_m, rows4_r, ALU.mult), VE)
        block.vector(v_4mr)
        mr4 = VE.n

        def t_4bc(t):
            t.wait_ge(VE.h, mr4)
            ins = None
            for th in range(2):
                _w(t, bank_free[th])
                _w(t, bank_free[2 + th])
                t.matmul(PB[th][:, 0:512], ones4r,
                         rows4_m[:, th * 512:th * 512 + 512], start=True, stop=True)
                ins = t.matmul(PB[2 + th][:, 0:512], ones4r,
                               rows4_r[:, th * 512:th * 512 + 512], start=True, stop=True)
            _bump(ins, PE)
        block.tensor(t_4bc)
        bc4 = PE.n

        x2ld4 = [0] * NKC
        outw = [0] * NKC
        fg_prev = [0, 0]
        for oc in range(NKC):
            r = oc % 2

            def s_x2r4(sp, oc=oc, r=r):
                sp.wait_ge(X2S.h, x2w[oc])
                if oc < 2:
                    sp.wait_ge(VE.h, down_done)
                else:
                    sp.wait_ge(PL.h, fg_prev[r])
                d = sp.dma_start(out=x2st[r], in_=ap["x2s"][oc])
                x2ld4[oc] = _bump(d, LD, 16)
            block.sync(s_x2r4)

            def v_fin(v, oc=oc, r=r):
                v.wait_ge(PE.h, bc4)
                if oc >= 1:
                    v.wait_ge(PL.h, fg_prev[1 - r])
                if oc >= 2:
                    v.wait_ge(SO.h, outw[oc - 2])
                do = down_fm[:, oc * 1024:oc * 1024 + 1024]
                ins = None
                for th in range(2):
                    t_ = t4s[th]
                    v.tensor_tensor(t_, do[:, th * 512:th * 512 + 512],
                                    PB[2 + th][:, 0:512], ALU.mult)
                    ins = v.tensor_tensor(outst[r][:, th * 512:th * 512 + 512],
                                          t_, PB[th][:, 0:512], ALU.subtract)
                _bump(ins, VE)
            block.vector(v_fin)
            fv = VE.n

            def g_fin(g, oc=oc, r=r, fv=fv):
                g.wait_ge(VE.h, fv)
                g.wait_ge(LD.h, x2ld4[oc])
                _bump(g.tensor_tensor(outst[r], outst[r], x2st[r], ALU.add), PL)
            block.gpsimd(g_fin)
            fg_prev[r] = PL.n
            fg = PL.n

            def s_outw(sp, oc=oc, r=r, fg=fg):
                sp.wait_ge(PL.h, fg)
                dst = ap["out"][oc].rearrange("p i tau -> p (i tau)")
                d = sp.dma_start(out=dst, in_=outst[r])
                outw[oc] = _bump(d, SO, 16)
            block.sync(s_outw)

        final_so = SO.n

        def s_fini(sp):
            sp.wait_ge(SO.h, final_so)
        block.sync(s_fini)

    return nc


def _emit_dbg(block, dbuf, srcs, sems, SO):
    vals = [(s, s.n) for s in sems]

    def s_dbg(sp):
        for s, v in vals:
            if v > 0:
                sp.wait_ge(s.h, v)
        off = 0
        if dbuf is not None:
            for src in srcs:
                w = src.shape[-1]
                _bump(sp.dma_start(out=dbuf[:, off:off + w], in_=src), SO, 16)
                off += w
        if SO.n > 0:
            sp.wait_ge(SO.h, SO.n)
    block.sync(s_dbg)


_ENG_ORDER = {"EngineType.SP": 0, "EngineType.Activation": 1, "EngineType.DVE": 2,
              "EngineType.PE": 3, "EngineType.Pool": 4}


REORDER = [0]


def _build(upto=9, dbg=()):
    nc = _build_inner(upto, dbg)
    if not REORDER[0]:
        return nc
    fn = nc.main_func
    blocks = list(fn.blocks)
    first, last = blocks[0], blocks[-1]
    mid = blocks[1:-1]

    def key(b):
        for i in b.instructions:
            e = str(i.engine)
            if e in _ENG_ORDER:
                return _ENG_ORDER[e]
        return 9
    out = [first]
    W = 120
    for w in range(0, len(mid), W):
        out.extend(sorted(mid[w:w + W], key=key))
    out.append(last)
    fn.blocks = out
    return nc



_NC_CACHE = {}


def _get_nc(upto=9, dbg=()):
    key = (upto, tuple(dbg))
    if key not in _NC_CACHE:
        _NC_CACHE[key] = _build(upto, dbg)
    return _NC_CACHE[key]


def _make_in_maps(x, Wqkv, bqkv, Wd, bd, Wg, Wu, Wdn):
    shared = _prep_shared(Wqkv, bqkv, Wd, bd, Wg, Wu, Wdn)
    wkeys = ["wq", "wk", "wv", "wd", "wg", "wu", "wdn", "bias", "ident"]
    in_maps = []
    for c in range(NCORE):
        b, h = c // 2, c % 2
        core = _prep_core(x[b], h, shared)
        m = {k: shared[k] for k in wkeys}
        m.update({k: core[k] for k in ("xt", "xo", "trig", "mask")})
        in_maps.append(m)
    return in_maps


def kernel(L0_attn_hidden_states, L0_attn_W_qkv, L0_attn_b_qkv, L0_attn_W_dense,
           L0_attn_b_dense, L0_ffn_w_gate, L0_ffn_w_up, L0_ffn_w_down,
           _upto=9, _dbg=()):
    x = np.asarray(L0_attn_hidden_states, dtype=np.float32)
    in_maps = _make_in_maps(
        x,
        np.asarray(L0_attn_W_qkv, np.float32), np.asarray(L0_attn_b_qkv, np.float32),
        np.asarray(L0_attn_W_dense, np.float32), np.asarray(L0_attn_b_dense, np.float32),
        np.asarray(L0_ffn_w_gate, np.float32), np.asarray(L0_ffn_w_up, np.float32),
        np.asarray(L0_ffn_w_down, np.float32))
    nc = _get_nc(_upto, _dbg)
    res = run_bass_kernel_spmd(nc, in_maps, core_ids=list(range(NCORE)))
    if _upto < 9:
        return res
    out = np.zeros((B, S, D), dtype=np.float32)
    for c in range(NCORE):
        b, h = c // 2, c % 2
        o = np.asarray(res.results[c]["out"])
        for i in range(NQT):
            g = 2 * i + h
            out[b, 128 * g:128 * g + 128, :] = \
                o[:, :, i, :].transpose(2, 0, 1).reshape(128, D)
    return out
